# version 28
# baseline (speedup 1.0000x reference)
import contextlib
import numpy as np
import ml_dtypes
import concourse.bass as bass
import concourse.mybir as mybir
from concourse.bass_utils import run_bass_kernel_spmd

F32 = mybir.dt.float32
BF16 = mybir.dt.bfloat16
AF = mybir.ActivationFunctionType
ALU = mybir.AluOpType
AX = mybir.AxisListType.X
BFNP = ml_dtypes.bfloat16

C = 2048
NH = 16
G = 4
HD = 128
E = 32
EH = 64
CCN = C // 128
TB = 512
SCALE = float(np.float32(1.0) / np.float32(np.sqrt(np.float32(HD))))
EXPSHIFT = -20.0
EPS = 1e-5
NEGMAX = float(np.finfo(np.float32).max)

ENGS = ("sync", "scalar", "vector", "tensor", "gpsimd")


class Buf:
    __slots__ = ("ready", "cons")

    def __init__(self):
        self.ready = None
        self.cons = []

    def r(self):
        return [self.ready] if self.ready is not None else []

    def w(self):
        return list(self.cons) + self.r()

    def set_ready(self, rv):
        self.ready = rv
        self.cons = []

    def used(self, rv):
        if rv is not None:
            self.cons.append(rv)


class Plan:
    def __init__(self, nc):
        self.nc = nc
        self.recs = {e: [] for e in ENGS}
        self.sems = {}
        self.cnt = {}

    def _sem(self, name):
        if name not in self.sems:
            self.sems[name] = self.nc.alloc_semaphore(name)
            self.cnt[name] = 0
        return self.sems[name]

    def val(self, name):
        self._sem(name)
        return (name, self.cnt[name])

    def emit(self, eng, fn, waits=(), inc=None, inck=1):
        ws = tuple(w for w in waits if w is not None)
        iv = None
        if inc is not None:
            self._sem(inc)
            self.cnt[inc] += inck
            iv = (inc, inck)
        self.recs[eng].append((fn, ws, iv))
        return (inc, self.cnt[inc]) if inc is not None else None

    def block(self, name):
        recs_by = self.recs
        self.recs = {e: [] for e in ENGS}
        sems = self.sems
        with self.nc.Block(name, no_gpsimd_drain=True) as blk:
            for en in ENGS:
                recs = recs_by[en]
                if not recs:
                    continue

                def mk(recs=recs):
                    def body(eng):
                        hw = {}
                        for fn, ws, iv in recs:
                            for s, v in ws:
                                if hw.get(s, -1) < v:
                                    eng.wait_ge(sems[s], v)
                                    hw[s] = v
                            if fn is None:
                                continue
                            inst = fn(eng)
                            if iv is not None:
                                inst.then_inc(sems[iv[0]], iv[1])
                    return body
                getattr(blk, en)(mk())


def build_nc(T=4096, taps=None):
    taps = taps or []
    NTB = T // TB
    NT128 = T // 128
    nc = bass.Bass("TRN2", target_bir_lowering=False)
    p = Plan(nc)

    d_x = nc.dram_tensor("x", [T, C], F32, kind="ExternalInput").ap()
    d_xfm = nc.dram_tensor("x_fm", [C, T], F32, kind="ExternalInput").ap()
    d_wq = nc.dram_tensor("Wq2", [128, CCN * C], F32, kind="ExternalInput").ap()
    d_wk = nc.dram_tensor("Wk2", [128, CCN * 512], F32, kind="ExternalInput").ap()
    d_wv = nc.dram_tensor("Wv2", [128, CCN * 512], F32, kind="ExternalInput").ap()
    d_wo = nc.dram_tensor("Wo2", [128, CCN * C], F32, kind="ExternalInput").ap()
    d_bo = nc.dram_tensor("bo", [1, C], F32, kind="ExternalInput").ap()
    d_sim = nc.dram_tensor("sim", [C, E], F32, kind="ExternalInput").ap()
    d_gates = nc.dram_tensor("gates", [1, E], F32, kind="ExternalInput").ap()
    d_wv2 = nc.dram_tensor("wv2", [C, C], F32, kind="ExternalInput").ap()
    d_ow2 = nc.dram_tensor("ow2", [C, C], F32, kind="ExternalInput").ap()
    d_id = nc.dram_tensor("identw", [128, 128], BF16, kind="ExternalInput").ap()
    d_masks = nc.dram_tensor("masks", [128, 4 * TB], BF16, kind="ExternalInput").ap()
    d_lt32 = nc.dram_tensor("lt32", [E, E], F32, kind="ExternalInput").ap()
    d_out = nc.dram_tensor("out", [T, C], F32, kind="ExternalOutput").ap()

    s_rstd = nc.dram_tensor("rstd_s", [1, T], F32, kind="Internal").ap()
    s_nb = nc.dram_tensor("nb_s", [1, T], F32, kind="Internal").ap()
    s_q = nc.dram_tensor("q_s", [C, T], BF16, kind="Internal").ap()
    s_k = nc.dram_tensor("k_s", [512, T], BF16, kind="Internal").ap()
    s_v = nc.dram_tensor("v_s", [T, 512], BF16, kind="Internal").ap()
    s_y = nc.dram_tensor("y_s", [T, C], F32, kind="Internal").ap()
    s_yn = nc.dram_tensor("yn_s", [T, C], BF16, kind="Internal").ap()
    s_seq = nc.dram_tensor("seq_s", [1, C], F32, kind="Internal").ap()
    s_lg = nc.dram_tensor("lg_s", [1, E], F32, kind="Internal").ap()
    s_fb = nc.dram_tensor("fb_s", [1, E], F32, kind="Internal").ap()

    d_taps = {}
    for tname, shape, dt in [("ctx", [128, (T // TB) * NH * TB], BF16),
                             ("ao", [T, C], F32),
                             ("xn", [T, C], BF16), ("q", [C, T], BF16),
                             ("k", [512, T], BF16), ("v", [T, 512], BF16),
                             ("y", [T, C], F32), ("yn", [T, C], BF16),
                             ("gat", [1, 256], F32)]:
        if tname in taps:
            d_taps[tname] = nc.dram_tensor("tap_" + tname, shape, dt,
                                           kind="ExternalOutput").ap()

    ones1 = nc.alloc_sbuf_tensor("ones1", [1, 128], F32)
    ones128 = nc.alloc_sbuf_tensor("ones128", [128, 1], F32)
    onesb = nc.alloc_sbuf_tensor("onesb", [128, 1], BF16)
    onesqf = nc.alloc_sbuf_tensor("onesqf", [128, 128], F32)
    c_qscale = nc.alloc_sbuf_tensor("c_qscale", [128, 1], F32)
    c_es = nc.alloc_sbuf_tensor("c_es", [128, 1], F32)
    c_mhalf = nc.alloc_sbuf_tensor("c_mhalf", [128, 1], F32)
    c_mone = nc.alloc_sbuf_tensor("c_mone", [128, 1], F32)
    ident = nc.alloc_sbuf_tensor("ident", [128, 128], BF16)
    masks_sb = nc.alloc_sbuf_tensor("masks_sb", [128, 4 * TB], BF16)
    bobc = nc.alloc_sbuf_tensor("bobc", [128, C], BF16)
    lt32_sb = nc.alloc_sbuf_tensor("lt32_sb", [E, E], F32)
    gates_sb = nc.alloc_sbuf_tensor("gates_sb", [1, E], F32)

    bufs = {}

    def B(name):
        if name not in bufs:
            bufs[name] = Buf()
        return bufs[name]

    tiny = {}

    def Tiny4(name):
        if name not in tiny:
            tiny[name] = (nc.alloc_sbuf_tensor("t4_" + name, [128, 4], F32), Buf())
        return tiny[name]

    def Tiny(name):
        if name not in tiny:
            tiny[name] = (nc.alloc_sbuf_tensor("tn_" + name, [128, 1], F32), Buf())
        return tiny[name]

    for _pb2 in (0, 1):
        for _nm in ("tsum", "tsq", "tmean", "tm2", "tva", "tvb", "tln", "trstd", "tnb"):
            Tiny4(f"{_nm}{_pb2}")
    for _tag in ("l2", "l3", "l4"):
        for _par in (0, 1):
            for _nm in ("sum", "sq", "mean", "m2", "va", "vb", "ln", "rstd", "nb"):
                Tiny(f"{_tag}_{_nm}{_par}")

    def ln_stats(tag, par, xt_ap, xt_buf, sq_ap, sq_buf, in_place_sq=False):
        t_sum, b_sum = Tiny(f"{tag}_sum{par}")
        t_sq, b_sq = Tiny(f"{tag}_sq{par}")
        t_mean, b_mean = Tiny(f"{tag}_mean{par}")
        t_m2, b_m2 = Tiny(f"{tag}_m2{par}")
        t_va, b_va = Tiny(f"{tag}_va{par}")
        t_vb, b_vb = Tiny(f"{tag}_vb{par}")
        t_ln, b_ln = Tiny(f"{tag}_ln{par}")
        t_rstd, b_rstd = Tiny(f"{tag}_rstd{par}")
        t_nb, b_nb = Tiny(f"{tag}_nb{par}")

        rv = p.emit("vector", lambda e, o=t_sum, i=xt_ap: e.reduce_sum(o[:], i, axis=AX),
                    waits=xt_buf.r() + b_sum.w(), inc="dve")
        xt_buf.used(rv); b_sum.set_ready(rv)
        sq_waits = (b_sum.r() if in_place_sq else [])
        rv = p.emit("scalar", lambda e, o=sq_ap, i=xt_ap, a=t_sq:
                    e.activation(o, i, AF.Square, accum_out=a[:]),
                    waits=xt_buf.r() + b_sq.w() + sq_buf.w() + sq_waits, inc="act")
        xt_buf.used(rv); b_sq.set_ready(rv); sq_buf.set_ready(rv)
        rv = p.emit("vector", lambda e, o=t_mean, i=t_sum:
                    e.tensor_scalar_mul(o[:], i[:], 1.0 / C),
                    waits=b_sum.r() + b_mean.w(), inc="dve")
        b_sum.used(rv); b_mean.set_ready(rv)
        rv = p.emit("vector", lambda e, o=t_m2, i=t_mean:
                    e.tensor_mul(o[:], i[:], i[:]),
                    waits=b_mean.r() + b_m2.w(), inc="dve")
        b_mean.used(rv); b_m2.set_ready(rv)
        rv = p.emit("vector", lambda e, o=t_va, i=t_sq:
                    e.tensor_scalar(o[:], i[:], 1.0 / C, EPS, ALU.mult, ALU.add),
                    waits=b_sq.r() + b_va.w(), inc="dve")
        b_sq.used(rv); b_va.set_ready(rv)
        rv = p.emit("vector", lambda e, o=t_vb, a=t_va, b=t_m2:
                    e.tensor_sub(o[:], a[:], b[:]),
                    waits=b_va.r() + b_m2.r() + b_vb.w(), inc="dve")
        b_va.used(rv); b_m2.used(rv); b_vb.set_ready(rv)
        rv = p.emit("scalar", lambda e, o=t_ln, i=t_vb: e.activation(o[:], i[:], AF.Ln),
                    waits=b_vb.r() + b_ln.w(), inc="act")
        b_vb.used(rv); b_ln.set_ready(rv)
        rv = p.emit("scalar", lambda e, o=t_rstd, i=t_ln:
                    e.activation(o[:], i[:], AF.Exp, scale=c_mhalf[:]),
                    waits=b_ln.r() + b_rstd.w() + [cst_rv], inc="act")
        b_ln.used(rv); b_rstd.set_ready(rv)
        rv = p.emit("vector", lambda e, o=t_nb, m=t_mean, r=t_rstd:
                    e.scalar_tensor_tensor(o[:], m[:], -1.0, r[:], ALU.mult, ALU.mult),
                    waits=b_mean.r() + b_rstd.r() + b_nb.w(), inc="dve")
        b_mean.used(rv); b_rstd.used(rv); b_nb.set_ready(rv)
        return t_rstd, b_rstd, t_nb, b_nb

    for t_, v_ in [(ones1, 1.0), (ones128, 1.0), (onesb, 1.0), (onesqf, 1.0),
                   (c_qscale, SCALE),
                   (c_es, EXPSHIFT), (c_mhalf, -0.5), (c_mone, -1.0)]:
        p.emit("vector", lambda e, t=t_, v=v_: e.memset(t[:], v), inc="dve")
    cst_rv = p.val("dve")

    for dst, src in [(ident, d_id), (lt32_sb, d_lt32), (gates_sb, d_gates)]:
        p.emit("sync", lambda e, d=dst, s=src: e.dma_start(out=d[:], in_=s),
               inc="cld", inck=16)
    p.emit("sync", lambda e: e.dma_start(out=masks_sb[:], in_=d_masks),
           inc="cld", inck=16)
    with contextlib.ExitStack() as stk1:
        with contextlib.ExitStack() as stk0:
            bo_sb = stk0.enter_context(nc.sbuf_tensor("bo_sb", [1, C], F32))
            bobc_ps = stk0.enter_context(nc.psum_tensor("bobc_ps", [128, C], F32))
            p.emit("sync", lambda e: e.dma_start(out=bo_sb[:], in_=d_bo), inc="cld", inck=16)
            cld_rv = p.val("cld")
            bo_rv = cld_rv
            for s4 in range(4):
                rv = p.emit("tensor", lambda e, s4=s4: e.matmul(
                    bobc_ps[:, s4 * 512:(s4 + 1) * 512], ones1[:],
                    bo_sb[:, s4 * 512:(s4 + 1) * 512], start=True, stop=True),
                    waits=[bo_rv, cst_rv], inc="pe")
            rv = p.emit("scalar", lambda e: e.activation(bobc[:], bobc_ps[:], AF.Copy),
                        waits=[rv], inc="act")
            B("bobc").set_ready(rv)

        wq_sb = stk1.enter_context(nc.sbuf_tensor("wq_sb", [128, CCN * C], BF16))
        wk_sb = stk1.enter_context(nc.sbuf_tensor("wk_sb", [128, CCN * 512], BF16))
        wv_sb = stk1.enter_context(nc.sbuf_tensor("wv_sb", [128, CCN * 512], BF16))
        x_t = [stk1.enter_context(nc.sbuf_tensor(f"x_t{i}", [128, C], F32)) for i in range(4)]
        xnfm = [stk1.enter_context(nc.sbuf_tensor(f"xnfm{i}", [128, CCN * TB], BF16)) for i in range(2)]
        stage = [stk1.enter_context(nc.sbuf_tensor(f"stage{i}", [128, TB], BF16)) for i in range(4)]
        rows_sb = [stk1.enter_context(nc.sbuf_tensor(f"rows_sb{i}", [1, 2 * TB], F32)) for i in range(2)]
        bcs1_sb = [stk1.enter_context(nc.sbuf_tensor(f"bcs1_sb{i}", [128, 2 * TB], F32)) for i in range(2)]
        xfm_t = [stk1.enter_context(nc.sbuf_tensor(f"xfm_t{i}", [128, TB], F32)) for i in range(4)]
        qkv_ps = [stk1.enter_context(nc.psum_tensor(f"qkv_ps{i}", [128, TB], F32)) for i in range(4)]
        bc1_ps_t = stk1.enter_context(nc.psum_tensor("bc1_ps", [128, 2 * TB], F32))
        bc1_ps = [bc1_ps_t, bc1_ps_t]


        def emit_A(b):
            pb2 = b % 2
            ts4, bs4 = Tiny4(f"tsum{pb2}")
            tq4, bq4 = Tiny4(f"tsq{pb2}")
            tm4, bm4 = Tiny4(f"tmean{pb2}")
            t24, b24 = Tiny4(f"tm2{pb2}")
            ta4, ba4 = Tiny4(f"tva{pb2}")
            tb4, bb4 = Tiny4(f"tvb{pb2}")
            tl4, bl4 = Tiny4(f"tln{pb2}")
            tr4, br4 = Tiny4(f"trstd{pb2}")
            tn4, bn4 = Tiny4(f"tnb{pb2}")
            for i in range(4):
                ti = b * 4 + i
                par = ti % 4
                xb = B(f"x_t{par}")
                rv = p.emit("scalar", lambda e, ti=ti, par=par: e.dma_start(
                    out=x_t[par][:], in_=d_x[ti * 128:(ti + 1) * 128, :]),
                    waits=xb.w(), inc=f"xld{par}", inck=16)
                xb.set_ready(rv)
                rv = p.emit("vector", lambda e, i=i, par=par, o=ts4: e.reduce_sum(
                    o[:, i:i + 1], x_t[par][:], axis=AX),
                    waits=xb.r() + (bs4.w() if i == 0 else []), inc="dve")
                xb.used(rv)
                if i == 3:
                    bs4.set_ready(rv)
                rv = p.emit("scalar", lambda e, i=i, par=par, o=tq4: e.activation(
                    x_t[par][:], x_t[par][:], AF.Square, accum_out=o[:, i:i + 1]),
                    waits=xb.w() + (bq4.w() if i == 0 else []), inc="act")
                xb.set_ready(rv)
                if i == 3:
                    bq4.set_ready(rv)
            rv = p.emit("vector", lambda e, o=tm4, i=ts4: e.tensor_scalar_mul(
                o[:], i[:], 1.0 / C), waits=bs4.r() + bm4.w(), inc="dve")
            bs4.used(rv); bm4.set_ready(rv)
            rv = p.emit("vector", lambda e, o=t24, i=tm4: e.tensor_mul(
                o[:], i[:], i[:]), waits=bm4.r() + b24.w(), inc="dve")
            bm4.used(rv); b24.set_ready(rv)
            rv = p.emit("vector", lambda e, o=ta4, i=tq4: e.tensor_scalar(
                o[:], i[:], 1.0 / C, EPS, ALU.mult, ALU.add),
                waits=bq4.r() + ba4.w(), inc="dve")
            bq4.used(rv); ba4.set_ready(rv)
            rv = p.emit("vector", lambda e, o=tb4, a=ta4, bb=t24: e.tensor_sub(
                o[:], a[:], bb[:]), waits=ba4.r() + b24.r() + bb4.w(), inc="dve")
            ba4.used(rv); b24.used(rv); bb4.set_ready(rv)
            rv = p.emit("scalar", lambda e, o=tl4, i=tb4: e.activation(
                o[:], i[:], AF.Ln), waits=bb4.r() + bl4.w(), inc="act")
            bb4.used(rv); bl4.set_ready(rv)
            rv = p.emit("scalar", lambda e, o=tr4, i=tl4: e.activation(
                o[:], i[:], AF.Exp, scale=c_mhalf[:]),
                waits=bl4.r() + br4.w() + [cst_rv], inc="act")
            bl4.used(rv); br4.set_ready(rv)
            rv = p.emit("vector", lambda e, o=tn4, m=tm4, r=tr4: e.scalar_tensor_tensor(
                o[:], m[:], -1.0, r[:], ALU.mult, ALU.mult),
                waits=bm4.r() + br4.r() + bn4.w(), inc="dve")
            bm4.used(rv); br4.used(rv); bn4.set_ready(rv)
            for i in range(4):
                ti = b * 4 + i
                par = ti % 4
                rv = p.emit("gpsimd", lambda e, ti=ti, i=i, r=tr4: e.dma_start(
                    out=s_rstd[:, ti * 128:(ti + 1) * 128].rearrange("1 p -> p 1"),
                    in_=r[:, i:i + 1]), waits=br4.r(), inc=f"rsta{par}", inck=16)
                br4.used(rv)
                rv = p.emit("gpsimd", lambda e, ti=ti, i=i, nb=tn4: e.dma_start(
                    out=s_nb[:, ti * 128:(ti + 1) * 128].rearrange("1 p -> p 1"),
                    in_=nb[:, i:i + 1]), waits=bn4.r(), inc=f"rstb{par}", inck=16)
                bn4.used(rv)
            rowst_at_block[b] = [p.val(f"rsta{q_}") for q_ in range(4)] + \
                                [p.val(f"rstb{q_}") for q_ in range(4)]

        rowst_at_block = {}
        fm_ready_at = {}

        def emit_B(b):
            pb = b % 2
            rwb = B(f"rows{pb}")
            rv = p.emit("sync", lambda e, b=b, pb=pb: e.dma_start(
                out=rows_sb[pb][0:1, 0:TB], in_=s_rstd[:, b * TB:(b + 1) * TB]),
                waits=rowst_at_block[b] + (rwb.w() if True else []), inc=f"rld{pb}", inck=16)
            rv = p.emit("sync", lambda e, b=b, pb=pb: e.dma_start(
                out=rows_sb[pb][0:1, TB:2 * TB], in_=s_nb[:, b * TB:(b + 1) * TB]),
                waits=rowst_at_block[b], inc=f"rld{pb}", inck=16)
            rwb.set_ready(p.val(f"rld{pb}"))
            bcb = B("bc1")
            rv = p.emit("tensor", lambda e, pb=pb: e.matmul(
                bc1_ps[pb][:, 0:TB], ones1[:], rows_sb[pb][0:1, 0:TB],
                start=True, stop=True),
                waits=rwb.r() + bcb.w() + [cst_rv] + B("bobc").r(), inc="pe")
            rwb.used(rv)
            rv = p.emit("tensor", lambda e, pb=pb: e.matmul(
                bc1_ps[pb][:, TB:2 * TB], ones1[:], rows_sb[pb][0:1, TB:2 * TB],
                start=True, stop=True),
                waits=rwb.r() + [cst_rv], inc="pe")
            rwb.used(rv); bcb.set_ready(rv)
            bsb = B(f"bcs1_{pb}")
            rv = p.emit("scalar", lambda e, pb=pb: e.activation(
                bcs1_sb[pb][:], bc1_ps[pb][:], AF.Copy),
                waits=bcb.r() + bsb.w(), inc="act")
            bcb.used(rv); bsb.set_ready(rv)

            fmb = B(f"xnfm{pb}")
            fm_ready = []
            for cc in range(CCN):
                xfb = B(f"xfm{cc % 4}")
                rv = p.emit("sync", lambda e, b=b, cc=cc: e.dma_start(
                    out=xfm_t[cc % 4][:],
                    in_=d_xfm[cc * 128:(cc + 1) * 128, b * TB:(b + 1) * TB]),
                    waits=xfb.w(), inc=f"xfl{cc % 4}", inck=16)
                xfb.set_ready(rv)
                rv = p.emit("vector", lambda e, cc=cc, pb=pb: e.tensor_mul(
                    xfm_t[cc % 4][:], xfm_t[cc % 4][:], bcs1_sb[pb][:, 0:TB]),
                    waits=xfb.w() + bsb.r(), inc="dve")
                bsb.used(rv); xfb.set_ready(rv)
                rv = p.emit("vector", lambda e, cc=cc, pb=pb: e.tensor_add(
                    xnfm[pb][:, cc * TB:(cc + 1) * TB], xfm_t[cc % 4][:],
                    bcs1_sb[pb][:, TB:2 * TB]),
                    waits=xfb.r() + bsb.r() + (fmb.w() if cc == 0 else []), inc="dve")
                xfb.used(rv); bsb.used(rv)
                fm_ready.append(rv)
            fmb.set_ready(p.val("dve"))
            fm_ready_at[b] = fm_ready

        p.emit("gpsimd", lambda e: e.dma_start(out=wk_sb[:], in_=d_wk),
               waits=[cld_rv] + B("bobc").r(), inc="wldk", inck=16)
        p.emit("gpsimd", lambda e: e.dma_start(out=wv_sb[:], in_=d_wv),
               waits=[cld_rv] + B("bobc").r(), inc="wldv", inck=16)
        p.emit("gpsimd", lambda e: e.dma_start(out=wq_sb[:], in_=d_wq),
               waits=[cld_rv] + B("bobc").r(), inc="wldq", inck=16)
        wld_by_kind = {"q": p.val("wldq"), "k": p.val("wldk"), "v": p.val("wldv")}
        emit_A(0)
        emit_A(1)
        emit_B(0)
        for b in range(NTB):
            pb = b % 2
            if b + 1 < NTB:
                emit_B(b + 1)
            if b + 2 < NTB:
                emit_A(b + 2)
            fm_ready = fm_ready_at[b]
            fmb = B(f"xnfm{pb}")
            jobs = [("k", i) for i in range(G)] + [("v", i) for i in range(4)] + \
                   [("q", i) for i in range(NH)]
            for jn, (kind, idx) in enumerate(jobs):
                ps = qkv_ps[jn % 4]
                psb = B(f"qkv_ps{jn % 4}")
                rv = None
                for cc in range(CCN):
                    if kind == "q":
                        def mm(e, cc=cc, idx=idx, ps=ps, pb=pb):
                            return e.matmul(ps[:],
                                            wq_sb[:, cc * C + idx * 128: cc * C + (idx + 1) * 128],
                                            xnfm[pb][:, cc * TB:(cc + 1) * TB],
                                            start=(cc == 0), stop=(cc == CCN - 1))
                    elif kind == "k":
                        def mm(e, cc=cc, idx=idx, ps=ps, pb=pb):
                            return e.matmul(ps[:],
                                            wk_sb[:, cc * 512 + idx * 128: cc * 512 + (idx + 1) * 128],
                                            xnfm[pb][:, cc * TB:(cc + 1) * TB],
                                            start=(cc == 0), stop=(cc == CCN - 1))
                    else:
                        def mm(e, cc=cc, idx=idx, ps=ps, pb=pb):
                            return e.matmul(ps[:],
                                            xnfm[pb][:, cc * TB + idx * 128: cc * TB + (idx + 1) * 128],
                                            wv_sb[:, cc * 512:(cc + 1) * 512],
                                            start=(cc == 0), stop=(cc == CCN - 1))
                    last = cc == CCN - 1
                    rv = p.emit("tensor", mm,
                                waits=(psb.w() if cc == 0 else []) + [fm_ready[cc]] +
                                      ([wld_by_kind[kind]] if cc == 0 else []),
                                inc=("pe" if last else None))
                psb.set_ready(rv)
                fmb.used(rv)
                stg = B(f"stage{jn % 4}")
                if kind == "q":
                    def ev(e, jn=jn, ps=ps):
                        return e.activation(stage[jn % 4][:], ps[:], AF.Identity,
                                            scale=c_qscale[:])
                else:
                    def ev(e, jn=jn, ps=ps):
                        return e.activation(stage[jn % 4][:], ps[:], AF.Copy)
                rv = p.emit("scalar", ev, waits=psb.r() + stg.w() + [cst_rv], inc="act")
                psb.used(rv); stg.set_ready(rv)
                if kind == "q":
                    dst = s_q[idx * 128:(idx + 1) * 128, b * TB:(b + 1) * TB]
                elif kind == "k":
                    dst = s_k[idx * 128:(idx + 1) * 128, b * TB:(b + 1) * TB]
                else:
                    dst = s_v[b * TB + idx * 128: b * TB + (idx + 1) * 128, :]
                rv = p.emit("scalar", lambda e, dst=dst, jn=jn: e.dma_start(
                    out=dst, in_=stage[jn % 4][:]), waits=stg.r(),
                    inc=f"qkvst{jn % 4}", inck=16)
                stg.used(rv)

        qkvst_rv = [p.val(f"qkvst{i}") for i in range(4)]
        p.block("phase1")

    if any(t in d_taps for t in ("q", "k", "v")):
        for tname, src in [("q", s_q), ("k", s_k), ("v", s_v)]:
            if tname in d_taps:
                p.emit("sync", lambda e, tn=tname, src=src: e.dma_start(
                    out=d_taps[tn][:, :], in_=src[:, :]),
                    waits=qkvst_rv, inc="tapst", inck=16)
        p.emit("sync", None, waits=[p.val("tapst")])
        p.block("taps_qkv")

    with contextlib.ExitStack() as stk2:
        k_sb = stk2.enter_context(nc.sbuf_tensor("k_sb", [128, G * T], BF16))
        v_sb = stk2.enter_context(nc.sbuf_tensor("v_sb", [128, NT128 * 512], BF16))
        wo_sb = stk2.enter_context(nc.sbuf_tensor("wo_sb", [128, CCN * C], BF16))
        q_t = [stk2.enter_context(nc.sbuf_tensor(f"q_t{i}", [128, TB], BF16)) for i in range(2)]
        pt2 = [[stk2.enter_context(nc.sbuf_tensor(f"pt2_{s}_{i}", [128, 2 * TB], BF16))
                for i in range(2)] for s in range(2)]
        pacc2 = [stk2.enter_context(nc.sbuf_tensor(f"pacc2_{s}", [128, 2 * TB], F32))
                 for s in range(2)]
        bc_sb = [stk2.enter_context(nc.sbuf_tensor(f"bc_sb{i}", [128, TB], F32)) for i in range(2)]
        ctx_sb = stk2.enter_context(nc.sbuf_tensor("ctx_sb", [128, NH * TB], BF16))
        ctx_stg = stk2.enter_context(nc.sbuf_tensor("ctx_stg", [128, 4 * TB], BF16))
        attn_sb = stk2.enter_context(nc.sbuf_tensor("attn_sb", [128, C], F32))
        sq2_bf = stk2.enter_context(nc.sbuf_tensor("sq2_bf", [128, C], BF16))
        x2_t = stk2.enter_context(nc.sbuf_tensor("x2_t", [128, C], F32))

        sp2 = [stk2.enter_context(nc.psum_tensor(f"sp2_{i}", [128, 2 * TB], F32)) for i in range(2)]
        ctx_ps = [stk2.enter_context(nc.psum_tensor(f"ctx_ps{i}", [128, TB], F32)) for i in range(2)]
        tail_ps = stk2.enter_context(nc.psum_tensor("tail_ps", [128, TB], F32))
        wo_ps = stk2.enter_context(nc.psum_tensor("wo_ps", [128, TB], F32))

        for g in range(G):
            p.emit("sync", lambda e, g=g: e.dma_start(
                out=k_sb[:, g * T:(g + 1) * T], in_=s_k[g * 128:(g + 1) * 128, :]),
                waits=qkvst_rv, inc="kld", inck=16)
        B("k_sb").set_ready(p.val("kld"))
        for n_ in range(NT128):
            p.emit("sync", lambda e, n_=n_: e.dma_start(
                out=v_sb[:, n_ * 512:(n_ + 1) * 512],
                in_=s_v[n_ * 128:(n_ + 1) * 128, :]),
                waits=qkvst_rv, inc="vld", inck=16)
        B("v_sb").set_ready(p.val("vld"))
        p.emit("gpsimd", lambda e: e.dma_start(out=wo_sb[:], in_=d_wo),
               inc="wld2", inck=16)
        B("wo_sb").set_ready(p.val("wld2"))

        x2b = B("x2")
        rv = p.emit("gpsimd", lambda e: e.dma_start(
            out=x2_t[:], in_=d_x[0:128, :]), waits=x2b.w(), inc="x2ld", inck=16)
        x2b.set_ready(rv)

        def emit_qload(J, h):
            qb = B(f"q_t{h % 2}")
            rv = p.emit("sync", lambda e, J=J, h=h: e.dma_start(
                out=q_t[h % 2][:], in_=s_q[h * 128:(h + 1) * 128, J * TB:(J + 1) * TB]),
                waits=qkvst_rv + qb.w(), inc=f"qld{h % 2}", inck=16)
            qb.set_ready(rv)

        def emit_tail(J, h, s):
            pab = B(f"pacc2_{s}")
            tpb = B("tail_ps")
            rv = None
            for lane in range(2):
                rv = p.emit("tensor", lambda e, s=s, lane=lane: e.matmul(
                    tail_ps[:], onesqf[:],
                    pacc2[s][:, lane * TB:(lane + 1) * TB],
                    start=(lane == 0), stop=(lane == 1)),
                    waits=pab.r() + (tpb.w() if lane == 0 else []) + [cst_rv],
                    inc=("pe" if lane == 1 else None))
                pab.used(rv)
            tpb.set_ready(rv)
            bsb = B(f"bc_sb{s}")
            rv = p.emit("vector", lambda e, s=s: e.reciprocal(
                bc_sb[s][:], tail_ps[:]),
                waits=tpb.r() + bsb.w(), inc="dve")
            tpb.used(rv); bsb.set_ready(rv)
            cxb = B(f"ctx_ps{h % 2}")
            if h < 4:
                dst = ctx_stg[:, h * TB:(h + 1) * TB]
                ctxh = B(f"ctxstg_h{h}")
            else:
                dst = ctx_sb[:, h * TB:(h + 1) * TB]
                ctxh = B(f"ctx_h{h}")
            rv = p.emit("vector", lambda e, dst=dst, h=h, s=s: e.tensor_mul(
                dst, ctx_ps[h % 2][:], bc_sb[s][:]),
                waits=cxb.r() + bsb.r() + ctxh.w(), inc="dve")
            cxb.used(rv); bsb.used(rv); ctxh.set_ready(rv)

        emit_qload(0, 0)
        emit_qload(0, 1)
        wo_pending = []

        def drain_one():
            if wo_pending:
                wo_pending.pop(0)()

        def emit_epilogue(J, qs):
            ab = B("attn")
            if "ao" in d_taps:
                rv = p.emit("sync", lambda e, J=J, qs=qs: e.dma_start(
                    out=d_taps["ao"][J * TB + qs * 128: J * TB + (qs + 1) * 128, :],
                    in_=attn_sb[:]), waits=ab.r(), inc="tapst", inck=16)
                ab.used(rv)
            t_rstd, b_rstd, t_nb, b_nb = ln_stats(
                "l2", qs % 2, attn_sb[:], ab, sq2_bf[:], B("sq2_bf"))
            rv = p.emit("scalar", lambda e, r=t_rstd, nb=t_nb: e.activation(
                attn_sb[:], attn_sb[:], AF.Identity, bias=nb[:], scale=r[:]),
                waits=ab.w() + b_rstd.r() + b_nb.r(),
                inc="act")
            b_rstd.used(rv); b_nb.used(rv); ab.set_ready(rv)
            x2b = B("x2")
            rv = p.emit("vector", lambda e: e.tensor_add(
                attn_sb[:], attn_sb[:], x2_t[:]),
                waits=x2b.r() + ab.w(), inc="dve")
            x2b.used(rv); ab.set_ready(rv)
            nj, nqs = (J, qs + 1) if qs < 3 else (J + 1, 0)
            if nj < NTB:
                rv = p.emit("gpsimd", lambda e, nj=nj, nqs=nqs: e.dma_start(
                    out=x2_t[:],
                    in_=d_x[nj * TB + nqs * 128: nj * TB + (nqs + 1) * 128, :]),
                    waits=x2b.w(), inc="x2ld", inck=16)
                x2b.set_ready(rv)
            rv = p.emit("gpsimd", lambda e, J=J, qs=qs: e.dma_start(
                out=s_y[J * TB + qs * 128: J * TB + (qs + 1) * 128, :],
                in_=attn_sb[:]),
                waits=ab.r(), inc="yst", inck=16)
            ab.used(rv)
            if "y" in d_taps:
                rv = p.emit("sync", lambda e, J=J, qs=qs: e.dma_start(
                    out=d_taps["y"][J * TB + qs * 128: J * TB + (qs + 1) * 128, :],
                    in_=attn_sb[:]), waits=ab.r(), inc="tapst", inck=16)
                ab.used(rv)

        def mk_wo_chain(J, qs, ds):
            def go():
                wob = B("wo_ps")
                rv = None
                for cc in range(CCN):
                    last = cc == CCN - 1
                    rv = p.emit("tensor", lambda e, cc=cc, qs=qs, ds=ds: e.matmul(
                        wo_ps[:],
                        ctx_sb[:, cc * TB + qs * 128: cc * TB + (qs + 1) * 128],
                        wo_sb[:, cc * C + ds * 512: cc * C + (ds + 1) * 512],
                        start=(cc == 0), stop=last),
                        waits=(wob.w() if cc == 0 else []) + B(f"ctx_h{cc}").r() +
                              (B("wo_sb").r() if cc == 0 else []),
                        inc=("pe" if last else None))
                    B(f"ctx_h{cc}").used(rv)
                wob.set_ready(rv)
                ab = B("attn")
                rv = p.emit("vector", lambda e, ds=ds: e.tensor_add(
                    attn_sb[:, ds * 512:(ds + 1) * 512], wo_ps[:],
                    bobc[:, ds * 512:(ds + 1) * 512]),
                    waits=wob.r() + B("bobc").r() + (ab.w() if ds == 0 else []),
                    inc="dve")
                wob.used(rv)
                if ds == 3:
                    ab.set_ready(rv)
                    emit_epilogue(J, qs)
            return go

        for J in range(NTB):
            nP = 2 * (J + 1)
            for hp in range(0, NH, 2):
                heads = (hp, hp + 1)

                def emit_spair(i, h, s):
                    qb = B(f"q_t{h % 2}")
                    g = h // 4
                    rv = None
                    for half in range(2):
                        kt = 2 * i + half
                        hb = B(f"sp{s}h{half}")
                        rv = p.emit("tensor", lambda e, i=i, kt=kt, g=g, h=h, half=half,
                                    s=s: e.matmul(
                            sp2[s][:, half * TB:(half + 1) * TB],
                            k_sb[:, g * T + kt * 128: g * T + (kt + 1) * 128],
                            q_t[h % 2][:], start=True, stop=True),
                            waits=hb.w() + qb.r() + B("k_sb").r(), inc="pe")
                        qb.used(rv); hb.set_ready(rv)
                    return rv

                def emit_exp(i, h, s):
                    ptb = B(f"pt2_{s}_{i % 2}")
                    hb0, hb1 = B(f"sp{s}h0"), B(f"sp{s}h1")
                    rv = p.emit("scalar", lambda e, i=i, s=s: e.activation(
                        pt2[s][i % 2][:], sp2[s][:], AF.Exp, bias=c_es[:]),
                        waits=hb0.r() + hb1.r() + ptb.w() + [cst_rv], inc="act")
                    hb0.used(rv); hb1.used(rv); ptb.set_ready(rv)
                    if i >= nP - 2:
                        mi = i - (nP - 2)
                        rv = p.emit("vector", lambda e, i=i, mi=mi, s=s: e.tensor_mul(
                            pt2[s][i % 2][:], pt2[s][i % 2][:],
                            masks_sb[:, mi * 2 * TB:(mi + 1) * 2 * TB]),
                            waits=ptb.w() + [cld_rv], inc="dve")
                        ptb.set_ready(rv)
                    return pt2[s][i % 2], ptb

                def emit_acc(i, s, peff, peffb):
                    pa = B(f"pacc2_{s}")
                    eng = "vector" if s == 0 else "gpsimd"
                    cnm = "dve" if s == 0 else "gpe"
                    if i == 0:
                        rv = p.emit(eng, lambda e, s=s, peff=peff: e.tensor_copy(
                            pacc2[s][:], peff[:]),
                            waits=peffb.r() + pa.w(), inc=cnm)
                    else:
                        rv = p.emit(eng, lambda e, s=s, peff=peff: e.tensor_tensor(
                            pacc2[s][:], pacc2[s][:], peff[:], ALU.add),
                            waits=peffb.r() + pa.w(), inc=cnm)
                    peffb.used(rv); pa.set_ready(rv)

                def emit_pv(i, h, s, peff, peffb):
                    cxb = B(f"ctx_ps{h % 2}")
                    g = h // 4
                    rv = None
                    for half in range(2):
                        kt = 2 * i + half
                        first = i == 0 and half == 0
                        last = i == nP - 1 and half == 1
                        rv = p.emit("tensor", lambda e, kt=kt, g=g, h=h, half=half,
                                    peff=peff, first=first, last=last: e.matmul(
                            ctx_ps[h % 2][:],
                            v_sb[:, kt * 512 + g * 128: kt * 512 + (g + 1) * 128],
                            peff[:, half * TB:(half + 1) * TB],
                            start=first, stop=last),
                            waits=(cxb.w() if first else []) + peffb.r() + B("v_sb").r(),
                            inc=("pe" if last else None))
                        peffb.used(rv)
                    if rv is not None:
                        cxb.set_ready(rv)

                prev = {0: None, 1: None}
                for i in range(nP):
                    for s, h in enumerate(heads):
                        emit_spair(i, h, s)
                    if i == 0 and hp > 0:
                        emit_tail(J, hp - 2, 0)
                        emit_tail(J, hp - 1, 1)
                    for s, h in enumerate(heads):
                        pe_pb = emit_exp(i, h, s)
                        emit_acc(i, s, *pe_pb)
                        if prev[s] is not None:
                            emit_pv(prev[s][0], h, s, prev[s][1], prev[s][2])
                        prev[s] = (i,) + pe_pb
                    for _ in range(2 if J <= 1 else 1):
                        drain_one()
                for s, h in enumerate(heads):
                    emit_pv(prev[s][0], h, s, prev[s][1], prev[s][2])
                for s, h in enumerate(heads):
                    if h + 2 < NH:
                        emit_qload(J, h + 2)
                    elif J + 1 < NTB:
                        emit_qload(J + 1, h + 2 - NH)
                drain_one()
            emit_tail(J, NH - 2, 0)
            emit_tail(J, NH - 1, 1)
            for pr in range(2):
                stgs = [B(f"ctxstg_h{2 * pr}"), B(f"ctxstg_h{2 * pr + 1}")]
                dsts = [B(f"ctx_h{2 * pr}"), B(f"ctx_h{2 * pr + 1}")]
                rv = p.emit("vector", lambda e, pr=pr: e.tensor_copy(
                    ctx_sb[:, 2 * pr * TB:(2 * pr + 2) * TB],
                    ctx_stg[:, 2 * pr * TB:(2 * pr + 2) * TB]),
                    waits=stgs[0].r() + stgs[1].r() + dsts[0].w() + dsts[1].w(),
                    inc="dve")
                for b_ in stgs:
                    b_.used(rv)
                for b_ in dsts:
                    b_.set_ready(rv)
            for qs in range(4):
                for ds in range(4):
                    wo_pending.append(mk_wo_chain(J, qs, ds))
        while wo_pending:
            drain_one()

        yst_rv = p.val("yst")
        p.block("phase2")

    with contextlib.ExitStack() as stk3o:
        rmb_sb = stk3o.enter_context(nc.sbuf_tensor("rmb_sb", [EH, E], F32))
        rtb_sb = stk3o.enter_context(nc.sbuf_tensor("rtb_sb", [EH, E], F32))
        with contextlib.ExitStack() as stk3a:
            y3_t = [stk3a.enter_context(nc.sbuf_tensor(f"y3_t{i}", [128, C], F32)) for i in range(2)]
            ln3_sb = [stk3a.enter_context(nc.sbuf_tensor(f"ln3_sb{i}", [128, C], F32)) for i in range(2)]
            yn_bf = [stk3a.enter_context(nc.sbuf_tensor(f"yn_bf{i}", [128, C], BF16)) for i in range(2)]
            sq3 = stk3a.enter_context(nc.sbuf_tensor("sq3", [128, C], F32))
            seq_ps = [stk3a.enter_context(nc.psum_tensor(f"seq_ps{i}", [1, 512], F32)) for i in range(4)]

            for ti in range(NT128):
                par = ti % 2
                yb = B(f"y3_{par}")
                rv = p.emit("sync", lambda e, ti=ti, par=par: e.dma_start(
                    out=y3_t[par][:], in_=s_y[ti * 128:(ti + 1) * 128, :]),
                    waits=[yst_rv] + yb.w(), inc=f"yld{par}", inck=16)
                yb.set_ready(rv)
                t_rstd, b_rstd, t_nb, b_nb = ln_stats(
                    "l3", par, y3_t[par][:], yb, sq3[:], B("sq3"))
                l3b = B(f"ln3_{par}")
                rv = p.emit("scalar", lambda e, par=par, r=t_rstd, nb=t_nb: e.activation(
                    ln3_sb[par][:], y3_t[par][:], AF.Identity, bias=nb[:], scale=r[:]),
                    waits=yb.r() + b_rstd.r() + b_nb.r() + l3b.w(), inc="act")
                yb.used(rv); b_rstd.used(rv); b_nb.used(rv); l3b.set_ready(rv)
                ynb = B(f"ynbf{par}")
                rv = p.emit("vector", lambda e, par=par: e.tensor_copy(
                    yn_bf[par][:], ln3_sb[par][:]),
                    waits=l3b.r() + ynb.w(), inc="dve")
                l3b.used(rv); ynb.set_ready(rv)
                rv = p.emit("gpsimd", lambda e, ti=ti, par=par: e.dma_start(
                    out=s_yn[ti * 128:(ti + 1) * 128, :], in_=yn_bf[par][:]),
                    waits=ynb.r(), inc=f"ynst{par}", inck=16)
                ynb.used(rv)
                if "yn" in d_taps:
                    rv = p.emit("sync", lambda e, ti=ti, par=par: e.dma_start(
                        out=d_taps["yn"][ti * 128:(ti + 1) * 128, :], in_=yn_bf[par][:]),
                        waits=ynb.r(), inc="tapst", inck=16)
                    ynb.used(rv)
                for s4 in range(4):
                    rv = p.emit("tensor", lambda e, ti=ti, par=par, s4=s4: e.matmul(
                        seq_ps[s4][:], ones128[:],
                        ln3_sb[par][:, s4 * 512:(s4 + 1) * 512],
                        start=(ti == 0), stop=(ti == NT128 - 1)),
                        waits=l3b.r() + [cst_rv], inc=("pe" if s4 == 3 else None))
                    l3b.used(rv)
            ynst_rv = [p.val("ynst0"), p.val("ynst1")]
            seq_done = p.val("pe")

            seq_row = stk3a.enter_context(nc.sbuf_tensor("seq_row", [1, C], F32))
            sqg = stk3a.enter_context(nc.sbuf_tensor("sqg", [1, C], F32))
            sim_sb = stk3a.enter_context(nc.sbuf_tensor("sim_sb", [128, CCN * E], F32))
            sim_sq = stk3a.enter_context(nc.sbuf_tensor("sim_sq", [128, CCN * E], F32))
            seq_fm = stk3a.enter_context(nc.sbuf_tensor("seq_fm", [128, CCN], F32))
            g_pool = stk3a.enter_context(nc.sbuf_tensor("g_pool", [1, 32 * E], F32))
            g32 = stk3a.enter_context(nc.sbuf_tensor("g32", [E, E], F32))
            e32 = stk3a.enter_context(nc.sbuf_tensor("e32", [E, E], F32))
            e32b = stk3a.enter_context(nc.sbuf_tensor("e32b", [E, E], F32))
            r32 = stk3a.enter_context(nc.sbuf_tensor("r32", [E, E], F32))
            lb_sb = stk3a.enter_context(nc.sbuf_tensor("lb_sb", [E, E], F32))
            lg_col = stk3a.enter_context(nc.sbuf_tensor("lg_col", [E, 1], F32))
            rk_col = stk3a.enter_context(nc.sbuf_tensor("rk_col", [E, 1], F32))
            fb_col = stk3a.enter_context(nc.sbuf_tensor("fb_col", [E, 1], F32))
            t11 = stk3a.enter_context(nc.sbuf_tensor("t11", [1, 4], F32))
            g_ps = stk3a.enter_context(nc.psum_tensor("g_ps", [128, 512], F32))
            gp32 = g_ps[0:E, 0:E]
            gp1e = g_ps[0:1, 0:E]
            gp64 = g_ps[0:EH, 0:E]

            def grow(i):
                return g_pool[:, i * E:(i + 1) * E]

            sqb = B("seq_row")
            for s4 in range(4):
                p.emit("scalar", lambda e, s4=s4: e.activation(
                    seq_row[:, s4 * 512:(s4 + 1) * 512], seq_ps[s4][:], AF.Copy),
                    waits=[seq_done], inc="act")
            sqb.set_ready(p.val("act"))
            ssb = B("ss")
            rv = p.emit("scalar", lambda e: e.activation(
                sqg[:], seq_row[:], AF.Square, accum_out=t11[:, 0:1]),
                waits=sqb.r() + ssb.w(), inc="act")
            sqb.used(rv); ssb.set_ready(rv)
            rv = p.emit("sync", lambda e: e.dma_start(out=s_seq[:], in_=seq_row[:]),
                        waits=sqb.r(), inc="gsq", inck=16)
            sqb.used(rv)
            seq_st = p.val("gsq")
            sfmb = B("seq_fm")
            for cc in range(CCN):
                p.emit("sync", lambda e, cc=cc: e.dma_start(
                    out=seq_fm[:, cc:cc + 1],
                    in_=s_seq[:, cc * 128:(cc + 1) * 128].rearrange("1 p -> p 1")),
                    waits=[seq_st] + (sfmb.w() if cc == 0 else []), inc="gfm", inck=16)
            sfmb.set_ready(p.val("gfm"))
            simb = B("sim_sb")
            for cc in range(CCN):
                p.emit("sync", lambda e, cc=cc: e.dma_start(
                    out=sim_sb[:, cc * E:(cc + 1) * E],
                    in_=d_sim[cc * 128:(cc + 1) * 128, :]),
                    inc="gsim", inck=16)
            simb.set_ready(p.val("gsim"))
            simq = B("sim_sq")
            rv = p.emit("scalar", lambda e: e.activation(sim_sq[:], sim_sb[:], AF.Square),
                        waits=simb.r() + simq.w(), inc="act")
            simb.used(rv); simq.set_ready(rv)
            csb = B("cs_ps")
            rv = None
            for cc in range(CCN):
                last = cc == CCN - 1
                rv = p.emit("tensor", lambda e, cc=cc: e.matmul(
                    gp1e, ones128[:], sim_sq[:, cc * E:(cc + 1) * E],
                    start=(cc == 0), stop=(cc == CCN - 1)),
                    waits=simq.r() + [cst_rv] + (csb.w() if cc == 0 else []),
                    inc=("pe" if last else None))
                simq.used(rv)
            csb.set_ready(rv)
            cs_sb = B("cs_sb")
            rv = p.emit("scalar", lambda e: e.activation(grow(0), gp1e, AF.Copy),
                        waits=csb.r() + cs_sb.w(), inc="act")
            csb.used(rv); cs_sb.set_ready(rv)
            rawb = B("raw_ps")
            rv = None
            for cc in range(CCN):
                last = cc == CCN - 1
                rv = p.emit("tensor", lambda e, cc=cc: e.matmul(
                    gp1e, seq_fm[:, cc:cc + 1], sim_sb[:, cc * E:(cc + 1) * E],
                    start=(cc == 0), stop=(cc == CCN - 1)),
                    waits=sfmb.r() + simb.r() + (rawb.w() + csb.w() if cc == 0 else []),
                    inc=("pe" if last else None))
                sfmb.used(rv); simb.used(rv)
            rawb.set_ready(rv)
            raw_sb = B("raw_sb")
            rv = p.emit("scalar", lambda e: e.activation(grow(1), gp1e, AF.Copy),
                        waits=rawb.r() + raw_sb.w(), inc="act")
            rawb.used(rv); raw_sb.set_ready(rv)
            lssb = B("lss")
            rv = p.emit("scalar", lambda e: e.activation(t11[:, 1:2], t11[:, 0:1], AF.Ln),
                        waits=ssb.r() + lssb.w(), inc="act")
            ssb.used(rv); lssb.set_ready(rv)
            lcsb = B("lcs")
            rv = p.emit("scalar", lambda e: e.activation(grow(2), grow(0), AF.Ln),
                        waits=cs_sb.r() + lcsb.w(), inc="act")
            cs_sb.used(rv); lcsb.set_ready(rv)
            ldenb = B("lden")
            rv = p.emit("vector", lambda e: e.tensor_scalar_add(grow(3), grow(2), t11[:, 1:2]),
                        waits=lssb.r() + lcsb.r() + ldenb.w(), inc="dve")
            lssb.used(rv); lcsb.used(rv); ldenb.set_ready(rv)
            rdenb = B("rden")
            rv = p.emit("scalar", lambda e: e.activation(
                grow(4), grow(3), AF.Exp, scale=c_mhalf[0:1, :]),
                waits=ldenb.r() + rdenb.w() + [cst_rv], inc="act")
            ldenb.used(rv); rdenb.set_ready(rv)
            affb = B("aff")
            rv = p.emit("vector", lambda e: e.tensor_mul(grow(5), grow(1), grow(4)),
                        waits=raw_sb.r() + rdenb.r() + affb.w(), inc="dve")
            raw_sb.used(rv); rdenb.used(rv); affb.set_ready(rv)
            sgb = B("sg")
            rv = p.emit("scalar", lambda e: e.activation(grow(6), gates_sb[:], AF.Sigmoid),
                        waits=[cld_rv] + sgb.w(), inc="act")
            sgb.set_ready(rv)
            lgb = B("lg")
            rv = p.emit("vector", lambda e: e.tensor_sub(grow(7), grow(5), grow(6)),
                        waits=affb.r() + sgb.r() + lgb.w(), inc="dve")
            affb.used(rv); sgb.used(rv); lgb.set_ready(rv)
            gdb = B("gd")
            rv = p.emit("vector", lambda e: e.tensor_scalar_max(grow(8), grow(7), 0.0),
                        waits=lgb.r() + gdb.w(), inc="dve")
            lgb.used(rv); gdb.set_ready(rv)
            mkb = B("mk")
            rv = p.emit("vector", lambda e: e.tensor_single_scalar(
                grow(9), grow(7), 0.0, ALU.is_gt),
                waits=lgb.r() + mkb.w(), inc="dve")
            lgb.used(rv); mkb.set_ready(rv)
            nab = B("na")
            rv = p.emit("vector", lambda e: e.reduce_sum(t11[:, 2:3], grow(9), axis=AX),
                        waits=mkb.r() + nab.w(), inc="dve")
            mkb.used(rv); nab.set_ready(rv)
            iab = B("ia")
            rv = p.emit("vector", lambda e: e.tensor_single_scalar(
                t11[:, 3:4], t11[:, 2:3], 0.5, ALU.is_lt),
                waits=nab.r() + iab.w(), inc="dve")
            nab.used(rv); iab.set_ready(rv)
            rv = p.emit("sync", lambda e: e.dma_start(out=s_lg[:], in_=grow(7)),
                        waits=lgb.r(), inc="glg", inck=16)
            lgb.used(rv)
            lg_st = p.val("glg")
            lcolb = B("lg_col")
            rv = p.emit("sync", lambda e: e.dma_start(
                out=lg_col[:], in_=s_lg.rearrange("1 e -> e 1")),
                waits=[lg_st] + lcolb.w(), inc="glgc", inck=16)
            lcolb.set_ready(rv)
            lbb = B("lb_ps")
            rv = p.emit("tensor", lambda e: e.matmul(
                gp32, ones1[:, 0:E], grow(7), start=True, stop=True),
                waits=lgb.r() + [cst_rv] + rawb.w() + csb.w(), inc="pe")
            lgb.used(rv); lbb.set_ready(rv)
            lbsb = B("lb_sb")
            rv = p.emit("scalar", lambda e: e.activation(lb_sb[:], gp32, AF.Copy),
                        waits=lbb.r() + lbsb.w(), inc="act")
            lbb.used(rv); lbsb.set_ready(rv)
            g32b = B("g32")
            rv = p.emit("vector", lambda e: e.tensor_single_scalar(
                g32[:], lb_sb[:], lg_col[:], ALU.is_gt),
                waits=lbsb.r() + lcolb.r() + g32b.w(), inc="dve")
            lbsb.used(rv); lcolb.used(rv); g32b.set_ready(rv)
            e32bf = B("e32")
            rv = p.emit("vector", lambda e: e.tensor_single_scalar(
                e32[:], lb_sb[:], lg_col[:], ALU.is_equal),
                waits=lbsb.r() + lcolb.r() + e32bf.w(), inc="dve")
            lbsb.used(rv); lcolb.used(rv); e32bf.set_ready(rv)
            e32bb = B("e32b")
            rv = p.emit("vector", lambda e: e.tensor_mul(e32b[:], e32[:], lt32_sb[:]),
                        waits=e32bf.r() + [cld_rv] + e32bb.w(), inc="dve")
            e32bf.used(rv); e32bb.set_ready(rv)
            r32b = B("r32")
            rv = p.emit("vector", lambda e: e.tensor_add(r32[:], g32[:], e32b[:]),
                        waits=g32b.r() + e32bb.r() + r32b.w(), inc="dve")
            g32b.used(rv); e32bb.used(rv); r32b.set_ready(rv)
            rkb = B("rk")
            rv = p.emit("vector", lambda e: e.reduce_sum(rk_col[:], r32[:], axis=AX),
                        waits=r32b.r() + rkb.w(), inc="dve")
            r32b.used(rv); rkb.set_ready(rv)
            fcb = B("fb_col")
            rv = p.emit("vector", lambda e: e.tensor_single_scalar(
                fb_col[:], rk_col[:], float(E // 2) - 0.5, ALU.is_lt),
                waits=rkb.r() + fcb.w(), inc="dve")
            rkb.used(rv); fcb.set_ready(rv)
            rv = p.emit("sync", lambda e: e.dma_start(
                out=s_fb.rearrange("1 e -> e 1"), in_=fb_col[:]),
                waits=fcb.r(), inc="gfb", inck=16)
            fcb.used(rv)
            fb_st = p.val("gfb")
            frb = B("fb_row")
            rv = p.emit("sync", lambda e: e.dma_start(out=grow(10), in_=s_fb[:]),
                        waits=[fb_st] + frb.w(), inc="gfbr", inck=16)
            frb.set_ready(rv)
            d1b = B("d1")
            rv = p.emit("vector", lambda e: e.tensor_sub(grow(11), grow(10), grow(9)),
                        waits=frb.r() + mkb.r() + d1b.w(), inc="dve")
            frb.used(rv); mkb.used(rv); d1b.set_ready(rv)
            d2b = B("d2")
            rv = p.emit("vector", lambda e: e.tensor_single_scalar(
                grow(12), grow(11), t11[:, 3:4], ALU.mult),
                waits=d1b.r() + iab.r() + d2b.w(), inc="dve")
            d1b.used(rv); iab.used(rv); d2b.set_ready(rv)
            mkfb = B("mkf")
            rv = p.emit("vector", lambda e: e.tensor_add(grow(13), grow(9), grow(12)),
                        waits=mkb.r() + d2b.r() + mkfb.w(), inc="dve")
            mkb.used(rv); d2b.used(rv); mkfb.set_ready(rv)
            z1b = B("z1")
            rv = p.emit("vector", lambda e: e.tensor_mul(grow(14), grow(8), grow(13)),
                        waits=gdb.r() + mkfb.r() + z1b.w(), inc="dve")
            gdb.used(rv); mkfb.used(rv); z1b.set_ready(rv)
            z2b = B("z2")
            rv = p.emit("vector", lambda e: e.tensor_scalar(
                grow(15), grow(13), -1.0, NEGMAX, ALU.add, ALU.mult),
                waits=mkfb.r() + z2b.w(), inc="dve")
            mkfb.used(rv); z2b.set_ready(rv)
            z3b = B("z3")
            rv = p.emit("vector", lambda e: e.tensor_add(grow(16), grow(14), grow(15)),
                        waits=z1b.r() + z2b.r() + z3b.w(), inc="dve")
            z1b.used(rv); z2b.used(rv); z3b.set_ready(rv)
            zmb = B("zm")
            rv = p.emit("vector", lambda e: e.reduce_max(t11[:, 0:1], grow(16), axis=AX),
                        waits=z3b.r() + ssb.w() + zmb.w(), inc="dve")
            z3b.used(rv); zmb.set_ready(rv)
            zmnb = B("zmn")
            rv = p.emit("vector", lambda e: e.tensor_scalar_mul(t11[:, 1:2], t11[:, 0:1], -1.0),
                        waits=zmb.r() + lssb.w() + zmnb.w(), inc="dve")
            zmb.used(rv); zmnb.set_ready(rv)
            ezb = B("ez")
            rv = p.emit("scalar", lambda e: e.activation(
                grow(17), grow(16), AF.Exp, bias=t11[:, 1:2], accum_out=t11[:, 2:3]),
                waits=z3b.r() + zmnb.r() + nab.w() + ezb.w(), inc="act")
            z3b.used(rv); zmnb.used(rv); ezb.set_ready(rv)
            zrb = B("zr")
            rv = p.emit("vector", lambda e: e.reciprocal(t11[:, 3:4], t11[:, 2:3]),
                        waits=ezb.r() + iab.w() + zrb.w(), inc="dve")
            ezb.used(rv); zrb.set_ready(rv)
            rtb = B("rt")
            rv = p.emit("vector", lambda e: e.tensor_single_scalar(
                grow(18), grow(17), t11[:, 3:4], ALU.mult),
                waits=ezb.r() + zrb.r() + rtb.w(), inc="dve")
            ezb.used(rv); zrb.used(rv); rtb.set_ready(rv)
            rmb = B("rm")
            rv = p.emit("vector", lambda e: e.tensor_mul(grow(19), grow(18), grow(13)),
                        waits=rtb.r() + mkfb.r() + rmb.w(), inc="dve")
            rtb.used(rv); mkfb.used(rv); rmb.set_ready(rv)
            rmbp = B("rmb_ps")
            rv = p.emit("tensor", lambda e: e.matmul(
                gp64, ones1[:, 0:EH], grow(19), start=True, stop=True),
                waits=rmb.r() + [cst_rv] + lbb.w(), inc="pe")
            rmb.used(rv); rmbp.set_ready(rv)
            rmsb = B("rmb_sb")
            rv = p.emit("scalar", lambda e: e.activation(rmb_sb[:], gp64, AF.Copy),
                        waits=rmbp.r() + rmsb.w(), inc="act")
            rmbp.used(rv); rmsb.set_ready(rv)
            rtbp = B("rtb_ps")
            rv = p.emit("tensor", lambda e: e.matmul(
                gp64, ones1[:, 0:EH], grow(18), start=True, stop=True),
                waits=rtb.r() + rmbp.w() + [cst_rv], inc="pe")
            rtb.used(rv); rtbp.set_ready(rv)
            rtsb = B("rtb_sb")
            rv = p.emit("scalar", lambda e: e.activation(rtb_sb[:], gp64, AF.Copy),
                        waits=rtbp.r() + rtsb.w(), inc="act")
            rtbp.used(rv); rtsb.set_ready(rv)
            if "gat" in d_taps:
                for gi, gv in enumerate([0, 5, 7, 9, 13, 10, 18, 19]):
                    p.emit("sync", lambda e, gi=gi, gv=gv: e.dma_start(
                        out=d_taps["gat"][:, gi * E:(gi + 1) * E], in_=grow(gv)),
                        waits=rmb.r() + rtb.r(), inc="tapst", inck=16)
                p.emit("sync", None, waits=[p.val("tapst")])
            p.block("phase3ab")

        weffcm = stk3o.enter_context(nc.sbuf_tensor("weffcm", [128, CCN * EH], BF16))
        wod_bf = stk3o.enter_context(nc.sbuf_tensor("wod_bf", [EH, C], BF16))
        with contextlib.ExitStack() as stk3c:
            wv_t = [stk3c.enter_context(nc.sbuf_tensor(f"wv_t{i}", [EH, C], F32)) for i in range(2)]
            ow_t = [stk3c.enter_context(nc.sbuf_tensor(f"ow_t{i}", [EH, C], F32)) for i in range(2)]
            wacc = [stk3c.enter_context(nc.sbuf_tensor(f"wacc{i}", [EH, C], F32)) for i in range(2)]
            oacc = [stk3c.enter_context(nc.sbuf_tensor(f"oacc{i}", [EH, C], F32)) for i in range(2)]
            weff_bf = stk3c.enter_context(nc.sbuf_tensor("weff_bf", [EH, C], BF16))
            wt_ps = stk3c.enter_context(nc.psum_tensor("wt_ps", [128, EH], BF16))

            for ei in range(E):
                par = ei % 2
                wvb = B(f"wv_{par}")
                rv = p.emit("sync", lambda e, ei=ei, par=par: e.dma_start(
                    out=wv_t[par][:], in_=d_wv2[ei * EH:(ei + 1) * EH, :]),
                    waits=wvb.w(), inc=f"wvld{par}", inck=16)
                wvb.set_ready(rv)
                owb = B(f"ow_{par}")
                rv = p.emit("sync", lambda e, ei=ei, par=par: e.dma_start(
                    out=ow_t[par][:], in_=d_ow2[ei * EH:(ei + 1) * EH, :]),
                    waits=owb.w(), inc=f"owld{par}", inck=16)
                owb.set_ready(rv)
                wa_in, wa_out = B(f"wacc{ei % 2}"), B(f"wacc{(ei + 1) % 2}")
                oa_in, oa_out = B(f"oacc{ei % 2}"), B(f"oacc{(ei + 1) % 2}")
                if ei == 0:
                    rv = p.emit("vector", lambda e, par=par: e.tensor_single_scalar(
                        wacc[1][:], wv_t[par][:], rmb_sb[:, 0:1], ALU.mult),
                        waits=wvb.r() + B("rmb_sb").r() + wa_out.w(), inc="dve")
                    wvb.used(rv); B("rmb_sb").used(rv); wa_out.set_ready(rv)
                    rv = p.emit("vector", lambda e, par=par: e.tensor_single_scalar(
                        oacc[1][:], ow_t[par][:], rtb_sb[:, 0:1], ALU.mult),
                        waits=owb.r() + B("rtb_sb").r() + oa_out.w(), inc="dve")
                    owb.used(rv); B("rtb_sb").used(rv); oa_out.set_ready(rv)
                else:
                    rv = p.emit("vector", lambda e, ei=ei, par=par: e.scalar_tensor_tensor(
                        wacc[(ei + 1) % 2][:], wv_t[par][:], rmb_sb[:, ei:ei + 1],
                        wacc[ei % 2][:], ALU.mult, ALU.add),
                        waits=wvb.r() + wa_in.r() + B("rmb_sb").r() + wa_out.w(), inc="dve")
                    wvb.used(rv); wa_in.used(rv); B("rmb_sb").used(rv); wa_out.set_ready(rv)
                    rv = p.emit("vector", lambda e, ei=ei, par=par: e.scalar_tensor_tensor(
                        oacc[(ei + 1) % 2][:], ow_t[par][:], rtb_sb[:, ei:ei + 1],
                        oacc[ei % 2][:], ALU.mult, ALU.add),
                        waits=owb.r() + oa_in.r() + B("rtb_sb").r() + oa_out.w(), inc="dve")
                    owb.used(rv); oa_in.used(rv); B("rtb_sb").used(rv); oa_out.set_ready(rv)
            wfin = B(f"wacc{E % 2}")
            ofin = B(f"oacc{E % 2}")
            wbfb = B("weff_bf")
            rv = p.emit("vector", lambda e: e.tensor_copy(weff_bf[:], wacc[E % 2][:]),
                        waits=wfin.r() + wbfb.w(), inc="dve")
            wfin.used(rv); wbfb.set_ready(rv)
            obfb = B("wod_bf")
            rv = p.emit("vector", lambda e: e.tensor_copy(wod_bf[:], oacc[E % 2][:]),
                        waits=ofin.r() + obfb.w(), inc="dve")
            ofin.used(rv); obfb.set_ready(rv)
            for cc in range(CCN):
                wtb = B("wt_ps")
                rv = p.emit("tensor", lambda e, cc=cc: e.transpose(
                    wt_ps[:], weff_bf[:, cc * 128:(cc + 1) * 128], ident[0:EH, 0:EH]),
                    waits=wbfb.r() + [cld_rv] + wtb.w(), inc="pe")
                wbfb.used(rv); wtb.set_ready(rv)
                rv = p.emit("scalar", lambda e, cc=cc: e.activation(
                    weffcm[:, cc * EH:(cc + 1) * EH], wt_ps[:], AF.Copy),
                    waits=wtb.r(), inc="act")
                wtb.used(rv)
            wcmb = B("weffcm")
            wcmb.set_ready(p.val("act"))
            p.block("phase3c")

        with contextlib.ExitStack() as stk3d:
            ynfm = [stk3d.enter_context(nc.sbuf_tensor(f"ynfm{i}", [128, CCN * TB], BF16)) for i in range(2)]
            comb_bf = [stk3d.enter_context(nc.sbuf_tensor(f"comb_bf{i}", [EH, TB], BF16)) for i in range(2)]
            ffn_sb = [stk3d.enter_context(nc.sbuf_tensor(f"ffn_sb{i}", [128, C], F32)) for i in range(2)]
            lnf_sb = stk3d.enter_context(nc.sbuf_tensor("lnf_sb", [128, C], F32))
            out_sb = [stk3d.enter_context(nc.sbuf_tensor(f"out_sb{i}", [128, C], F32)) for i in range(2)]
            y4_t = [stk3d.enter_context(nc.sbuf_tensor(f"y4_t{i}", [128, C], F32)) for i in range(2)]
            sq4 = stk3d.enter_context(nc.sbuf_tensor("sq4", [128, C], F32))
            comb_ps = [stk3d.enter_context(nc.psum_tensor(f"comb_ps{i}", [EH, TB], F32)) for i in range(2)]
            f_ps = [stk3d.enter_context(nc.psum_tensor(f"f_ps{i}", [128, 512], F32)) for i in range(2)]

            for b in range(NTB):
                pb = b % 2
                fmb = B(f"ynfm{pb}")
                for cc in range(CCN):
                    p.emit("sync", lambda e, b=b, cc=cc, pb=pb: e.dma_start(
                        out=ynfm[pb][:, cc * TB:(cc + 1) * TB],
                        in_=s_yn[b * TB:(b + 1) * TB, cc * 128:(cc + 1) * 128],
                        transpose=True),
                        waits=ynst_rv + (fmb.w() if cc == 0 else []),
                        inc=f"ynfl{pb}", inck=16)
                fmb.set_ready(p.val(f"ynfl{pb}"))
                cpb = B(f"comb_ps{pb}")
                rv = None
                for cc in range(CCN):
                    last = cc == CCN - 1
                    rv = p.emit("tensor", lambda e, cc=cc, pb=pb: e.matmul(
                        comb_ps[pb][:], weffcm[:, cc * EH:(cc + 1) * EH],
                        ynfm[pb][:, cc * TB:(cc + 1) * TB],
                        start=(cc == 0), stop=(cc == CCN - 1)),
                        waits=fmb.r() + wcmb.r() + (cpb.w() if cc == 0 else []),
                        inc=("pe" if last else None))
                fmb.used(rv); wcmb.used(rv)
                cpb.set_ready(rv)
                cbb = B(f"comb_bf{pb}")
                rv = p.emit("scalar", lambda e, pb=pb: e.activation(
                    comb_bf[pb][:], comb_ps[pb][:], AF.Copy),
                    waits=cpb.r() + cbb.w(), inc="act")
                cpb.used(rv); cbb.set_ready(rv)
                for ts in range(4):
                    par = ts % 2
                    ti = b * 4 + ts
                    y4b = B(f"y4_{par}")
                    rv = p.emit("gpsimd", lambda e, ti=ti, par=par: e.dma_start(
                        out=y4_t[par][:], in_=s_y[ti * 128:(ti + 1) * 128, :]),
                        waits=[yst_rv] + y4b.w(), inc=f"y4ld{par}", inck=16)
                    y4b.set_ready(rv)
                    fb_ = B(f"ffn{par}")
                    for ds in range(4):
                        fp = f_ps[(ts * 4 + ds) % 2]
                        fpb = B(f"f_ps{(ts * 4 + ds) % 2}")
                        rv = p.emit("tensor", lambda e, ts=ts, ds=ds, pb=pb, fp=fp: e.matmul(
                            fp[:], comb_bf[pb][:, ts * 128:(ts + 1) * 128],
                            wod_bf[:, ds * 512:(ds + 1) * 512], start=True, stop=True),
                            waits=cbb.r() + obfb.r() + fpb.w(), inc="pe")
                        cbb.used(rv); obfb.used(rv); fpb.set_ready(rv)
                        rv = p.emit("vector", lambda e, par=par, ds=ds, fp=fp: e.tensor_copy(
                            ffn_sb[par][:, ds * 512:(ds + 1) * 512], fp[:]),
                            waits=fpb.r() + (fb_.w() if ds == 0 else []), inc="dve")
                        fpb.used(rv)
                    fb_.set_ready(p.val("dve"))
                    t_rstd, b_rstd, t_nb, b_nb = ln_stats(
                        "l4", par, ffn_sb[par][:], fb_, sq4[:], B("sq4"))
                    lfb = B("lnf")
                    rv = p.emit("scalar", lambda e, par=par, r=t_rstd, nb=t_nb: e.activation(
                        lnf_sb[:], ffn_sb[par][:], AF.Identity, bias=nb[:], scale=r[:]),
                        waits=fb_.r() + b_rstd.r() + b_nb.r() + lfb.w(), inc="act")
                    fb_.used(rv); b_rstd.used(rv); b_nb.used(rv); lfb.set_ready(rv)
                    ob = B(f"out{par}")
                    rv = p.emit("vector", lambda e, par=par: e.tensor_add(
                        out_sb[par][:], lnf_sb[:], y4_t[par][:]),
                        waits=lfb.r() + y4b.r() + ob.w(), inc="dve")
                    lfb.used(rv); y4b.used(rv); ob.set_ready(rv)
                    rv = p.emit("gpsimd", lambda e, ti=ti, par=par: e.dma_start(
                        out=d_out[ti * 128:(ti + 1) * 128, :], in_=out_sb[par][:]),
                        waits=ob.r(), inc="outst", inck=16)
                    ob.used(rv)

            p.emit("sync", None, waits=[p.val("outst")])
            p.block("phase3d")
    return nc


def make_masks():
    m = np.zeros((128, 4 * TB), dtype=np.float32)
    for o in range(4):
        off = o * 128
        for pp in range(128):
            m[pp, o * TB:(o + 1) * TB] = (np.arange(TB) >= (pp + off)).astype(np.float32)
    return m.astype(BFNP)


def make_inputs_per_core(xb, Wq, Wk, Wv, Wo, bo, sim, gates, wv2, ow2):
    lt = np.tril(np.ones((E, E), np.float32), -1)
    def chunkw(W):
        W = np.asarray(W, dtype=np.float32)
        cc = W.shape[0] // 128
        return np.ascontiguousarray(
            np.transpose(W.reshape(cc, 128, W.shape[1]), (1, 0, 2)).reshape(128, -1))
    return {
        "x": np.ascontiguousarray(xb, dtype=np.float32),
        "x_fm": np.ascontiguousarray(np.asarray(xb, dtype=np.float32).T),
        "Wq2": chunkw(Wq),
        "Wk2": chunkw(Wk),
        "Wv2": chunkw(Wv),
        "Wo2": chunkw(Wo),
        "bo": np.ascontiguousarray(np.reshape(bo, (1, C)), dtype=np.float32),
        "sim": np.ascontiguousarray(sim, dtype=np.float32),
        "gates": np.ascontiguousarray(np.reshape(gates, (1, E)), dtype=np.float32),
        "wv2": np.ascontiguousarray(np.reshape(wv2, (E * EH, C)), dtype=np.float32),
        "ow2": np.ascontiguousarray(np.reshape(ow2, (E * EH, C)), dtype=np.float32),
        "identw": np.eye(128, dtype=BFNP),
        "masks": make_masks(),
        "lt32": lt,
    }


def kernel(L0_attn_x, L0_attn_Wq, L0_attn_Wk, L0_attn_Wv, L0_attn_Wo, L0_attn_bo,
           L0_ffn_sim_matrix, L0_ffn_gates, L0_ffn_w_q, L0_ffn_w_k, L0_ffn_w_v,
           L0_ffn_o_weights):
    x = np.asarray(L0_attn_x)
    nb = x.shape[0]
    nc = build_nc(T=x.shape[1])
    in_maps = []
    for b in range(nb):
        in_maps.append(make_inputs_per_core(
            x[b], np.asarray(L0_attn_Wq), np.asarray(L0_attn_Wk),
            np.asarray(L0_attn_Wv), np.asarray(L0_attn_Wo), np.asarray(L0_attn_bo),
            np.asarray(L0_ffn_sim_matrix), np.asarray(L0_ffn_gates),
            np.asarray(L0_ffn_w_v), np.asarray(L0_ffn_o_weights)))
    res = run_bass_kernel_spmd(nc, in_maps, core_ids=list(range(nb)))
    out = np.stack([np.asarray(r["out"]) for r in res.results], axis=0)
    return out.astype(np.float32)


# version 30
# speedup vs baseline: 1.0892x; 1.0892x over previous
import contextlib
import numpy as np
import ml_dtypes
import concourse.bass as bass
import concourse.mybir as mybir
from concourse.bass_utils import run_bass_kernel_spmd

F32 = mybir.dt.float32
BF16 = mybir.dt.bfloat16
AF = mybir.ActivationFunctionType
ALU = mybir.AluOpType
AX = mybir.AxisListType.X
BFNP = ml_dtypes.bfloat16

C = 2048
NH = 16
G = 4
HD = 128
E = 32
EH = 64
CCN = C // 128
TB = 512
SCALE = float(np.float32(1.0) / np.float32(np.sqrt(np.float32(HD))))
EXPSHIFT = -20.0
EPS = 1e-5
NEGMAX = float(np.finfo(np.float32).max)

ENGS = ("sync", "scalar", "vector", "tensor", "gpsimd")


class Buf:
    __slots__ = ("ready", "cons")

    def __init__(self):
        self.ready = None
        self.cons = []

    def r(self):
        return [self.ready] if self.ready is not None else []

    def w(self):
        return list(self.cons) + self.r()

    def set_ready(self, rv):
        self.ready = rv
        self.cons = []

    def used(self, rv):
        if rv is not None:
            self.cons.append(rv)


class Plan:
    def __init__(self, nc):
        self.nc = nc
        self.recs = {e: [] for e in ENGS}
        self.sems = {}
        self.cnt = {}

    def _sem(self, name):
        if name not in self.sems:
            self.sems[name] = self.nc.alloc_semaphore(name)
            self.cnt[name] = 0
        return self.sems[name]

    def val(self, name):
        self._sem(name)
        return (name, self.cnt[name])

    def emit(self, eng, fn, waits=(), inc=None, inck=1):
        ws = tuple(w for w in waits if w is not None)
        iv = None
        if inc is not None:
            self._sem(inc)
            self.cnt[inc] += inck
            iv = (inc, inck)
        self.recs[eng].append((fn, ws, iv))
        return (inc, self.cnt[inc]) if inc is not None else None

    def block(self, name):
        recs_by = self.recs
        self.recs = {e: [] for e in ENGS}
        sems = self.sems
        with self.nc.Block(name, no_gpsimd_drain=True) as blk:
            for en in ENGS:
                recs = recs_by[en]
                if not recs:
                    continue

                def mk(recs=recs):
                    def body(eng):
                        hw = {}
                        for fn, ws, iv in recs:
                            for s, v in ws:
                                if hw.get(s, -1) < v:
                                    eng.wait_ge(sems[s], v)
                                    hw[s] = v
                            if fn is None:
                                continue
                            inst = fn(eng)
                            if iv is not None:
                                inst.then_inc(sems[iv[0]], iv[1])
                    return body
                getattr(blk, en)(mk())


def build_nc(T=4096, taps=None):
    taps = taps or []
    NTB = T // TB
    NT128 = T // 128
    nc = bass.Bass("TRN2", target_bir_lowering=False)
    p = Plan(nc)

    d_x = nc.dram_tensor("x", [T, C], F32, kind="ExternalInput").ap()
    d_xfm = nc.dram_tensor("x_fm", [C, T], F32, kind="ExternalInput").ap()
    d_wq = nc.dram_tensor("Wq2", [128, CCN * C], F32, kind="ExternalInput").ap()
    d_wk = nc.dram_tensor("Wk2", [128, CCN * 512], F32, kind="ExternalInput").ap()
    d_wv = nc.dram_tensor("Wv2", [128, CCN * 512], F32, kind="ExternalInput").ap()
    d_wo = nc.dram_tensor("Wo2", [128, CCN * C], F32, kind="ExternalInput").ap()
    d_bo = nc.dram_tensor("bo", [1, C], F32, kind="ExternalInput").ap()
    d_sim = nc.dram_tensor("sim", [C, E], F32, kind="ExternalInput").ap()
    d_gates = nc.dram_tensor("gates", [1, E], F32, kind="ExternalInput").ap()
    d_wv2 = nc.dram_tensor("wv2", [C, C], F32, kind="ExternalInput").ap()
    d_ow2 = nc.dram_tensor("ow2", [C, C], F32, kind="ExternalInput").ap()
    d_id = nc.dram_tensor("identw", [128, 128], BF16, kind="ExternalInput").ap()
    d_masks = nc.dram_tensor("masks", [128, 4 * TB], BF16, kind="ExternalInput").ap()
    d_lt32 = nc.dram_tensor("lt32", [E, E], F32, kind="ExternalInput").ap()
    d_out = nc.dram_tensor("out", [T, C], F32, kind="ExternalOutput").ap()

    s_rstd = nc.dram_tensor("rstd_s", [1, T], F32, kind="Internal").ap()
    s_nb = nc.dram_tensor("nb_s", [1, T], F32, kind="Internal").ap()
    s_q = nc.dram_tensor("q_s", [C, T], BF16, kind="Internal").ap()
    s_k = nc.dram_tensor("k_s", [512, T], BF16, kind="Internal").ap()
    s_v = nc.dram_tensor("v_s", [T, 512], BF16, kind="Internal").ap()
    s_y = nc.dram_tensor("y_s", [T, C], F32, kind="Internal").ap()
    s_yn = nc.dram_tensor("yn_s", [T, C], BF16, kind="Internal").ap()
    s_seq = nc.dram_tensor("seq_s", [1, C], F32, kind="Internal").ap()
    s_lg = nc.dram_tensor("lg_s", [1, E], F32, kind="Internal").ap()
    s_fb = nc.dram_tensor("fb_s", [1, E], F32, kind="Internal").ap()

    d_taps = {}
    for tname, shape, dt in [("ctx", [128, (T // TB) * NH * TB], BF16),
                             ("ao", [T, C], F32),
                             ("xn", [T, C], BF16), ("q", [C, T], BF16),
                             ("k", [512, T], BF16), ("v", [T, 512], BF16),
                             ("y", [T, C], F32), ("yn", [T, C], BF16),
                             ("gat", [1, 256], F32)]:
        if tname in taps:
            d_taps[tname] = nc.dram_tensor("tap_" + tname, shape, dt,
                                           kind="ExternalOutput").ap()

    ones1 = nc.alloc_sbuf_tensor("ones1", [1, 128], F32)
    ones128 = nc.alloc_sbuf_tensor("ones128", [128, 1], F32)
    onesb = nc.alloc_sbuf_tensor("onesb", [128, 1], BF16)
    onesqf = nc.alloc_sbuf_tensor("onesqf", [128, 128], F32)
    c_qscale = nc.alloc_sbuf_tensor("c_qscale", [128, 1], F32)
    c_es = nc.alloc_sbuf_tensor("c_es", [128, 1], F32)
    c_mhalf = nc.alloc_sbuf_tensor("c_mhalf", [128, 1], F32)
    c_mone = nc.alloc_sbuf_tensor("c_mone", [128, 1], F32)
    ident = nc.alloc_sbuf_tensor("ident", [128, 128], BF16)
    masks_sb = nc.alloc_sbuf_tensor("masks_sb", [128, 4 * TB], BF16)
    bobc = nc.alloc_sbuf_tensor("bobc", [128, C], BF16)
    lt32_sb = nc.alloc_sbuf_tensor("lt32_sb", [E, E], F32)
    gates_sb = nc.alloc_sbuf_tensor("gates_sb", [1, E], F32)

    bufs = {}

    def B(name):
        if name not in bufs:
            bufs[name] = Buf()
        return bufs[name]

    tiny = {}

    def Tiny4(name):
        if name not in tiny:
            tiny[name] = (nc.alloc_sbuf_tensor("t4_" + name, [128, 4], F32), Buf())
        return tiny[name]

    def Tiny(name):
        if name not in tiny:
            tiny[name] = (nc.alloc_sbuf_tensor("tn_" + name, [128, 1], F32), Buf())
        return tiny[name]

    for _pb2 in (0, 1):
        for _nm in ("tsum", "tsq", "tmean", "tm2", "tva", "tvb", "tln", "trstd", "tnb"):
            Tiny4(f"{_nm}{_pb2}")
    for _tag in ("l2", "l3", "l4"):
        for _par in (0, 1):
            for _nm in ("sum", "sq", "mean", "m2", "va", "vb", "ln", "rstd", "nb"):
                Tiny(f"{_tag}_{_nm}{_par}")

    def ln_stats(tag, par, xt_ap, xt_buf, sq_ap, sq_buf, in_place_sq=False):
        t_sum, b_sum = Tiny(f"{tag}_sum{par}")
        t_sq, b_sq = Tiny(f"{tag}_sq{par}")
        t_mean, b_mean = Tiny(f"{tag}_mean{par}")
        t_m2, b_m2 = Tiny(f"{tag}_m2{par}")
        t_va, b_va = Tiny(f"{tag}_va{par}")
        t_vb, b_vb = Tiny(f"{tag}_vb{par}")
        t_ln, b_ln = Tiny(f"{tag}_ln{par}")
        t_rstd, b_rstd = Tiny(f"{tag}_rstd{par}")
        t_nb, b_nb = Tiny(f"{tag}_nb{par}")

        rv = p.emit("vector", lambda e, o=t_sum, i=xt_ap: e.reduce_sum(o[:], i, axis=AX),
                    waits=xt_buf.r() + b_sum.w(), inc="dve")
        xt_buf.used(rv); b_sum.set_ready(rv)
        sq_waits = (b_sum.r() if in_place_sq else [])
        rv = p.emit("scalar", lambda e, o=sq_ap, i=xt_ap, a=t_sq:
                    e.activation(o, i, AF.Square, accum_out=a[:]),
                    waits=xt_buf.r() + b_sq.w() + sq_buf.w() + sq_waits, inc="act")
        xt_buf.used(rv); b_sq.set_ready(rv); sq_buf.set_ready(rv)
        rv = p.emit("vector", lambda e, o=t_mean, i=t_sum:
                    e.tensor_scalar_mul(o[:], i[:], 1.0 / C),
                    waits=b_sum.r() + b_mean.w(), inc="dve")
        b_sum.used(rv); b_mean.set_ready(rv)
        rv = p.emit("vector", lambda e, o=t_m2, i=t_mean:
                    e.tensor_mul(o[:], i[:], i[:]),
                    waits=b_mean.r() + b_m2.w(), inc="dve")
        b_mean.used(rv); b_m2.set_ready(rv)
        rv = p.emit("vector", lambda e, o=t_va, i=t_sq:
                    e.tensor_scalar(o[:], i[:], 1.0 / C, EPS, ALU.mult, ALU.add),
                    waits=b_sq.r() + b_va.w(), inc="dve")
        b_sq.used(rv); b_va.set_ready(rv)
        rv = p.emit("vector", lambda e, o=t_vb, a=t_va, b=t_m2:
                    e.tensor_sub(o[:], a[:], b[:]),
                    waits=b_va.r() + b_m2.r() + b_vb.w(), inc="dve")
        b_va.used(rv); b_m2.used(rv); b_vb.set_ready(rv)
        rv = p.emit("scalar", lambda e, o=t_ln, i=t_vb: e.activation(o[:], i[:], AF.Ln),
                    waits=b_vb.r() + b_ln.w(), inc="act")
        b_vb.used(rv); b_ln.set_ready(rv)
        rv = p.emit("scalar", lambda e, o=t_rstd, i=t_ln:
                    e.activation(o[:], i[:], AF.Exp, scale=c_mhalf[:]),
                    waits=b_ln.r() + b_rstd.w() + [cst_rv], inc="act")
        b_ln.used(rv); b_rstd.set_ready(rv)
        rv = p.emit("vector", lambda e, o=t_nb, m=t_mean, r=t_rstd:
                    e.scalar_tensor_tensor(o[:], m[:], -1.0, r[:], ALU.mult, ALU.mult),
                    waits=b_mean.r() + b_rstd.r() + b_nb.w(), inc="dve")
        b_mean.used(rv); b_rstd.used(rv); b_nb.set_ready(rv)
        return t_rstd, b_rstd, t_nb, b_nb

    for t_, v_ in [(ones1, 1.0), (ones128, 1.0), (onesb, 1.0), (onesqf, 1.0),
                   (c_qscale, SCALE),
                   (c_es, EXPSHIFT), (c_mhalf, -0.5), (c_mone, -1.0)]:
        p.emit("vector", lambda e, t=t_, v=v_: e.memset(t[:], v), inc="dve")
    cst_rv = p.val("dve")

    for dst, src in [(ident, d_id), (lt32_sb, d_lt32), (gates_sb, d_gates)]:
        p.emit("sync", lambda e, d=dst, s=src: e.dma_start(out=d[:], in_=s),
               inc="cld", inck=16)
    p.emit("sync", lambda e: e.dma_start(out=masks_sb[:], in_=d_masks),
           inc="cld", inck=16)
    with contextlib.ExitStack() as stk1:
        with contextlib.ExitStack() as stk0:
            bo_sb = stk0.enter_context(nc.sbuf_tensor("bo_sb", [1, C], F32))
            bobc_ps = stk0.enter_context(nc.psum_tensor("bobc_ps", [128, C], F32))
            p.emit("sync", lambda e: e.dma_start(out=bo_sb[:], in_=d_bo), inc="cld", inck=16)
            cld_rv = p.val("cld")
            bo_rv = cld_rv
            for s4 in range(4):
                rv = p.emit("tensor", lambda e, s4=s4: e.matmul(
                    bobc_ps[:, s4 * 512:(s4 + 1) * 512], ones1[:],
                    bo_sb[:, s4 * 512:(s4 + 1) * 512], start=True, stop=True),
                    waits=[bo_rv, cst_rv], inc="pe")
            rv = p.emit("scalar", lambda e: e.activation(bobc[:], bobc_ps[:], AF.Copy),
                        waits=[rv], inc="act")
            B("bobc").set_ready(rv)

        wq_sb = stk1.enter_context(nc.sbuf_tensor("wq_sb", [128, CCN * C], BF16))
        wk_sb = stk1.enter_context(nc.sbuf_tensor("wk_sb", [128, CCN * 512], BF16))
        wv_sb = stk1.enter_context(nc.sbuf_tensor("wv_sb", [128, CCN * 512], BF16))
        x_t = [stk1.enter_context(nc.sbuf_tensor(f"x_t{i}", [128, C], F32)) for i in range(4)]
        xnfm = [stk1.enter_context(nc.sbuf_tensor(f"xnfm{i}", [128, CCN * TB], BF16)) for i in range(2)]
        stage = [stk1.enter_context(nc.sbuf_tensor(f"stage{i}", [128, TB], BF16)) for i in range(4)]
        rows_sb = [stk1.enter_context(nc.sbuf_tensor(f"rows_sb{i}", [1, 2 * TB], F32)) for i in range(2)]
        bcs1_sb = [stk1.enter_context(nc.sbuf_tensor(f"bcs1_sb{i}", [128, 2 * TB], F32)) for i in range(2)]
        xfm_t = [stk1.enter_context(nc.sbuf_tensor(f"xfm_t{i}", [128, TB], F32)) for i in range(4)]
        qkv_ps = [stk1.enter_context(nc.psum_tensor(f"qkv_ps{i}", [128, TB], F32)) for i in range(4)]
        bc1_ps_t = stk1.enter_context(nc.psum_tensor("bc1_ps", [128, 2 * TB], F32))
        bc1_ps = [bc1_ps_t, bc1_ps_t]


        def emit_A(b):
            pb2 = b % 2
            ts4, bs4 = Tiny4(f"tsum{pb2}")
            tq4, bq4 = Tiny4(f"tsq{pb2}")
            tm4, bm4 = Tiny4(f"tmean{pb2}")
            t24, b24 = Tiny4(f"tm2{pb2}")
            ta4, ba4 = Tiny4(f"tva{pb2}")
            tb4, bb4 = Tiny4(f"tvb{pb2}")
            tl4, bl4 = Tiny4(f"tln{pb2}")
            tr4, br4 = Tiny4(f"trstd{pb2}")
            tn4, bn4 = Tiny4(f"tnb{pb2}")
            for i in range(4):
                ti = b * 4 + i
                par = ti % 4
                xb = B(f"x_t{par}")
                rv = p.emit("scalar", lambda e, ti=ti, par=par: e.dma_start(
                    out=x_t[par][:], in_=d_x[ti * 128:(ti + 1) * 128, :]),
                    waits=xb.w(), inc=f"xld{par}", inck=16)
                xb.set_ready(rv)
                rv = p.emit("vector", lambda e, i=i, par=par, o=ts4: e.reduce_sum(
                    o[:, i:i + 1], x_t[par][:], axis=AX),
                    waits=xb.r() + (bs4.w() if i == 0 else []), inc="dve")
                xb.used(rv)
                if i == 3:
                    bs4.set_ready(rv)
                rv = p.emit("scalar", lambda e, i=i, par=par, o=tq4: e.activation(
                    x_t[par][:], x_t[par][:], AF.Square, accum_out=o[:, i:i + 1]),
                    waits=xb.w() + (bq4.w() if i == 0 else []), inc="act")
                xb.set_ready(rv)
                if i == 3:
                    bq4.set_ready(rv)
            rv = p.emit("vector", lambda e, o=tm4, i=ts4: e.tensor_scalar_mul(
                o[:], i[:], 1.0 / C), waits=bs4.r() + bm4.w(), inc="dve")
            bs4.used(rv); bm4.set_ready(rv)
            rv = p.emit("vector", lambda e, o=t24, i=tm4: e.tensor_mul(
                o[:], i[:], i[:]), waits=bm4.r() + b24.w(), inc="dve")
            bm4.used(rv); b24.set_ready(rv)
            rv = p.emit("vector", lambda e, o=ta4, i=tq4: e.tensor_scalar(
                o[:], i[:], 1.0 / C, EPS, ALU.mult, ALU.add),
                waits=bq4.r() + ba4.w(), inc="dve")
            bq4.used(rv); ba4.set_ready(rv)
            rv = p.emit("vector", lambda e, o=tb4, a=ta4, bb=t24: e.tensor_sub(
                o[:], a[:], bb[:]), waits=ba4.r() + b24.r() + bb4.w(), inc="dve")
            ba4.used(rv); b24.used(rv); bb4.set_ready(rv)
            rv = p.emit("scalar", lambda e, o=tl4, i=tb4: e.activation(
                o[:], i[:], AF.Ln), waits=bb4.r() + bl4.w(), inc="act")
            bb4.used(rv); bl4.set_ready(rv)
            rv = p.emit("scalar", lambda e, o=tr4, i=tl4: e.activation(
                o[:], i[:], AF.Exp, scale=c_mhalf[:]),
                waits=bl4.r() + br4.w() + [cst_rv], inc="act")
            bl4.used(rv); br4.set_ready(rv)
            rv = p.emit("vector", lambda e, o=tn4, m=tm4, r=tr4: e.scalar_tensor_tensor(
                o[:], m[:], -1.0, r[:], ALU.mult, ALU.mult),
                waits=bm4.r() + br4.r() + bn4.w(), inc="dve")
            bm4.used(rv); br4.used(rv); bn4.set_ready(rv)
            for i in range(4):
                ti = b * 4 + i
                par = ti % 4
                rv = p.emit("gpsimd", lambda e, ti=ti, i=i, r=tr4: e.dma_start(
                    out=s_rstd[:, ti * 128:(ti + 1) * 128].rearrange("1 p -> p 1"),
                    in_=r[:, i:i + 1]), waits=br4.r(), inc=f"rsta{par}", inck=16)
                br4.used(rv)
                rv = p.emit("gpsimd", lambda e, ti=ti, i=i, nb=tn4: e.dma_start(
                    out=s_nb[:, ti * 128:(ti + 1) * 128].rearrange("1 p -> p 1"),
                    in_=nb[:, i:i + 1]), waits=bn4.r(), inc=f"rstb{par}", inck=16)
                bn4.used(rv)
            rowst_at_block[b] = [p.val(f"rsta{q_}") for q_ in range(4)] + \
                                [p.val(f"rstb{q_}") for q_ in range(4)]

        rowst_at_block = {}
        fm_ready_at = {}

        def emit_B(b):
            pb = b % 2
            rwb = B(f"rows{pb}")
            rv = p.emit("sync", lambda e, b=b, pb=pb: e.dma_start(
                out=rows_sb[pb][0:1, 0:TB], in_=s_rstd[:, b * TB:(b + 1) * TB]),
                waits=rowst_at_block[b] + (rwb.w() if True else []), inc=f"rld{pb}", inck=16)
            rv = p.emit("sync", lambda e, b=b, pb=pb: e.dma_start(
                out=rows_sb[pb][0:1, TB:2 * TB], in_=s_nb[:, b * TB:(b + 1) * TB]),
                waits=rowst_at_block[b], inc=f"rld{pb}", inck=16)
            rwb.set_ready(p.val(f"rld{pb}"))
            bcb = B("bc1")
            rv = p.emit("tensor", lambda e, pb=pb: e.matmul(
                bc1_ps[pb][:, 0:TB], ones1[:], rows_sb[pb][0:1, 0:TB],
                start=True, stop=True),
                waits=rwb.r() + bcb.w() + [cst_rv] + B("bobc").r(), inc="pe")
            rwb.used(rv)
            rv = p.emit("tensor", lambda e, pb=pb: e.matmul(
                bc1_ps[pb][:, TB:2 * TB], ones1[:], rows_sb[pb][0:1, TB:2 * TB],
                start=True, stop=True),
                waits=rwb.r() + [cst_rv], inc="pe")
            rwb.used(rv); bcb.set_ready(rv)
            bsb = B(f"bcs1_{pb}")
            rv = p.emit("scalar", lambda e, pb=pb: e.activation(
                bcs1_sb[pb][:], bc1_ps[pb][:], AF.Copy),
                waits=bcb.r() + bsb.w(), inc="act")
            bcb.used(rv); bsb.set_ready(rv)

            fmb = B(f"xnfm{pb}")
            fm_ready = []
            for cc in range(CCN):
                xfb = B(f"xfm{cc % 4}")
                rv = p.emit("sync", lambda e, b=b, cc=cc: e.dma_start(
                    out=xfm_t[cc % 4][:],
                    in_=d_xfm[cc * 128:(cc + 1) * 128, b * TB:(b + 1) * TB]),
                    waits=xfb.w(), inc=f"xfl{cc % 4}", inck=16)
                xfb.set_ready(rv)
                rv = p.emit("vector", lambda e, cc=cc, pb=pb: e.tensor_mul(
                    xfm_t[cc % 4][:], xfm_t[cc % 4][:], bcs1_sb[pb][:, 0:TB]),
                    waits=xfb.w() + bsb.r(), inc="dve")
                bsb.used(rv); xfb.set_ready(rv)
                rv = p.emit("vector", lambda e, cc=cc, pb=pb: e.tensor_add(
                    xnfm[pb][:, cc * TB:(cc + 1) * TB], xfm_t[cc % 4][:],
                    bcs1_sb[pb][:, TB:2 * TB]),
                    waits=xfb.r() + bsb.r() + (fmb.w() if cc == 0 else []), inc="dve")
                xfb.used(rv); bsb.used(rv)
                fm_ready.append(rv)
            fmb.set_ready(p.val("dve"))
            fm_ready_at[b] = fm_ready

        p.emit("gpsimd", lambda e: e.dma_start(out=wk_sb[:], in_=d_wk),
               waits=[cld_rv] + B("bobc").r(), inc="wldk", inck=16)
        p.emit("gpsimd", lambda e: e.dma_start(out=wv_sb[:], in_=d_wv),
               waits=[cld_rv] + B("bobc").r(), inc="wldv", inck=16)
        p.emit("gpsimd", lambda e: e.dma_start(out=wq_sb[:], in_=d_wq),
               waits=[cld_rv] + B("bobc").r(), inc="wldq", inck=16)
        wld_by_kind = {"q": p.val("wldq"), "k": p.val("wldk"), "v": p.val("wldv")}
        emit_A(0)
        emit_A(1)
        emit_B(0)
        for b in range(NTB):
            pb = b % 2
            if b + 1 < NTB:
                emit_B(b + 1)
            if b + 2 < NTB:
                emit_A(b + 2)
            fm_ready = fm_ready_at[b]
            fmb = B(f"xnfm{pb}")
            jobs = [("k", i) for i in range(G)] + [("v", i) for i in range(4)] + \
                   [("q", i) for i in range(NH)]
            for jn, (kind, idx) in enumerate(jobs):
                ps = qkv_ps[jn % 4]
                psb = B(f"qkv_ps{jn % 4}")
                rv = None
                for cc in range(CCN):
                    if kind == "q":
                        def mm(e, cc=cc, idx=idx, ps=ps, pb=pb):
                            return e.matmul(ps[:],
                                            wq_sb[:, cc * C + idx * 128: cc * C + (idx + 1) * 128],
                                            xnfm[pb][:, cc * TB:(cc + 1) * TB],
                                            start=(cc == 0), stop=(cc == CCN - 1))
                    elif kind == "k":
                        def mm(e, cc=cc, idx=idx, ps=ps, pb=pb):
                            return e.matmul(ps[:],
                                            wk_sb[:, cc * 512 + idx * 128: cc * 512 + (idx + 1) * 128],
                                            xnfm[pb][:, cc * TB:(cc + 1) * TB],
                                            start=(cc == 0), stop=(cc == CCN - 1))
                    else:
                        def mm(e, cc=cc, idx=idx, ps=ps, pb=pb):
                            return e.matmul(ps[:],
                                            xnfm[pb][:, cc * TB + idx * 128: cc * TB + (idx + 1) * 128],
                                            wv_sb[:, cc * 512:(cc + 1) * 512],
                                            start=(cc == 0), stop=(cc == CCN - 1))
                    last = cc == CCN - 1
                    rv = p.emit("tensor", mm,
                                waits=(psb.w() if cc == 0 else []) + [fm_ready[cc]] +
                                      ([wld_by_kind[kind]] if cc == 0 else []),
                                inc=("pe" if last else None))
                psb.set_ready(rv)
                fmb.used(rv)
                stg = B(f"stage{jn % 4}")
                if kind == "q":
                    def ev(e, jn=jn, ps=ps):
                        return e.activation(stage[jn % 4][:], ps[:], AF.Identity,
                                            scale=c_qscale[:])
                else:
                    def ev(e, jn=jn, ps=ps):
                        return e.activation(stage[jn % 4][:], ps[:], AF.Copy)
                rv = p.emit("scalar", ev, waits=psb.r() + stg.w() + [cst_rv], inc="act")
                psb.used(rv); stg.set_ready(rv)
                if kind == "q":
                    dst = s_q[idx * 128:(idx + 1) * 128, b * TB:(b + 1) * TB]
                elif kind == "k":
                    dst = s_k[idx * 128:(idx + 1) * 128, b * TB:(b + 1) * TB]
                else:
                    dst = s_v[b * TB + idx * 128: b * TB + (idx + 1) * 128, :]
                rv = p.emit("scalar", lambda e, dst=dst, jn=jn: e.dma_start(
                    out=dst, in_=stage[jn % 4][:]), waits=stg.r(),
                    inc=f"qkvst{jn % 4}", inck=16)
                stg.used(rv)

        qkvst_rv = [p.val(f"qkvst{i}") for i in range(4)]
        p.block("phase1")

    if any(t in d_taps for t in ("q", "k", "v")):
        for tname, src in [("q", s_q), ("k", s_k), ("v", s_v)]:
            if tname in d_taps:
                p.emit("sync", lambda e, tn=tname, src=src: e.dma_start(
                    out=d_taps[tn][:, :], in_=src[:, :]),
                    waits=qkvst_rv, inc="tapst", inck=16)
        p.emit("sync", None, waits=[p.val("tapst")])
        p.block("taps_qkv")

    with contextlib.ExitStack() as stk2:
        k_sb = stk2.enter_context(nc.sbuf_tensor("k_sb", [128, G * T], BF16))
        v_sb = stk2.enter_context(nc.sbuf_tensor("v_sb", [128, NT128 * 512], BF16))
        wo_sb = stk2.enter_context(nc.sbuf_tensor("wo_sb", [128, CCN * C], BF16))
        q_t = [stk2.enter_context(nc.sbuf_tensor(f"q_t{i}", [128, TB], BF16)) for i in range(2)]
        pt2 = [[stk2.enter_context(nc.sbuf_tensor(f"pt2_{s}_{i}", [128, 2 * TB], BF16))
                for i in range(2)] for s in range(2)]
        pacc2 = [stk2.enter_context(nc.sbuf_tensor(f"pacc2_{s}", [128, 2 * TB], F32))
                 for s in range(2)]
        bc_sb = [stk2.enter_context(nc.sbuf_tensor(f"bc_sb{i}", [128, TB], F32)) for i in range(2)]
        ctx_sb = stk2.enter_context(nc.sbuf_tensor("ctx_sb", [128, NH * TB], BF16))
        ctx_stg = stk2.enter_context(nc.sbuf_tensor("ctx_stg", [128, 4 * TB], BF16))
        attn_sb = stk2.enter_context(nc.sbuf_tensor("attn_sb", [128, C], F32))
        sq2_bf = stk2.enter_context(nc.sbuf_tensor("sq2_bf", [128, C], BF16))
        x2_t = stk2.enter_context(nc.sbuf_tensor("x2_t", [128, C], F32))

        sp2 = [stk2.enter_context(nc.psum_tensor(f"sp2_{i}", [128, 2 * TB], F32)) for i in range(2)]
        ctx_ps = [stk2.enter_context(nc.psum_tensor(f"ctx_ps{i}", [128, TB], F32)) for i in range(2)]
        tail_ps = stk2.enter_context(nc.psum_tensor("tail_ps", [128, TB], F32))
        wo_ps = stk2.enter_context(nc.psum_tensor("wo_ps", [128, TB], F32))

        for g in range(G):
            p.emit("sync", lambda e, g=g: e.dma_start(
                out=k_sb[:, g * T:(g + 1) * T], in_=s_k[g * 128:(g + 1) * 128, :]),
                waits=qkvst_rv, inc="kld", inck=16)
        B("k_sb").set_ready(p.val("kld"))
        for n_ in range(NT128):
            p.emit("sync", lambda e, n_=n_: e.dma_start(
                out=v_sb[:, n_ * 512:(n_ + 1) * 512],
                in_=s_v[n_ * 128:(n_ + 1) * 128, :]),
                waits=qkvst_rv, inc="vld", inck=16)
        B("v_sb").set_ready(p.val("vld"))
        p.emit("gpsimd", lambda e: e.dma_start(out=wo_sb[:], in_=d_wo),
               inc="wld2", inck=16)
        B("wo_sb").set_ready(p.val("wld2"))

        x2b = B("x2")
        rv = p.emit("gpsimd", lambda e: e.dma_start(
            out=x2_t[:], in_=d_x[0:128, :]), waits=x2b.w(), inc="x2ld", inck=16)
        x2b.set_ready(rv)

        def emit_qload(J, h):
            qb = B(f"q_t{h % 2}")
            rv = p.emit("sync", lambda e, J=J, h=h: e.dma_start(
                out=q_t[h % 2][:], in_=s_q[h * 128:(h + 1) * 128, J * TB:(J + 1) * TB]),
                waits=qkvst_rv + qb.w(), inc=f"qld{h % 2}", inck=16)
            qb.set_ready(rv)

        def emit_tail(J, h, s):
            pab = B(f"pacc2_{s}")
            tpb = B("tail_ps")
            rv = None
            for lane in range(2):
                rv = p.emit("tensor", lambda e, s=s, lane=lane: e.matmul(
                    tail_ps[:], onesqf[:],
                    pacc2[s][:, lane * TB:(lane + 1) * TB],
                    start=(lane == 0), stop=(lane == 1)),
                    waits=pab.r() + (tpb.w() if lane == 0 else []) + [cst_rv],
                    inc=("pe" if lane == 1 else None))
                pab.used(rv)
            tpb.set_ready(rv)
            bsb = B(f"bc_sb{s}")
            rv = p.emit("scalar", lambda e, s=s: e.activation(
                bc_sb[s][:], tail_ps[:], AF.Ln),
                waits=tpb.r() + bsb.w(), inc="act")
            tpb.used(rv)
            rv = p.emit("scalar", lambda e, s=s: e.activation(
                bc_sb[s][:], bc_sb[s][:], AF.Exp, scale=c_mone[:]),
                waits=bsb.w() + [rv, cst_rv], inc="act")
            bsb.set_ready(rv)
            cxb = B(f"ctx_ps{h % 2}")
            if h < 4:
                dst = ctx_stg[:, h * TB:(h + 1) * TB]
                ctxh = B(f"ctxstg_h{h}")
            else:
                dst = ctx_sb[:, h * TB:(h + 1) * TB]
                ctxh = B(f"ctx_h{h}")
            rv = p.emit("vector", lambda e, dst=dst, h=h, s=s: e.tensor_mul(
                dst, ctx_ps[h % 2][:], bc_sb[s][:]),
                waits=cxb.r() + bsb.r() + ctxh.w(), inc="dve")
            cxb.used(rv); bsb.used(rv); ctxh.set_ready(rv)

        emit_qload(0, 0)
        emit_qload(0, 1)
        wo_pending = []

        def drain_one():
            if wo_pending:
                wo_pending.pop(0)()

        def emit_epilogue(J, qs):
            ab = B("attn")
            if "ao" in d_taps:
                rv = p.emit("sync", lambda e, J=J, qs=qs: e.dma_start(
                    out=d_taps["ao"][J * TB + qs * 128: J * TB + (qs + 1) * 128, :],
                    in_=attn_sb[:]), waits=ab.r(), inc="tapst", inck=16)
                ab.used(rv)
            t_rstd, b_rstd, t_nb, b_nb = ln_stats(
                "l2", qs % 2, attn_sb[:], ab, sq2_bf[:], B("sq2_bf"))
            rv = p.emit("scalar", lambda e, r=t_rstd, nb=t_nb: e.activation(
                attn_sb[:], attn_sb[:], AF.Identity, bias=nb[:], scale=r[:]),
                waits=ab.w() + b_rstd.r() + b_nb.r(),
                inc="act")
            b_rstd.used(rv); b_nb.used(rv); ab.set_ready(rv)
            x2b = B("x2")
            rv = p.emit("gpsimd", lambda e: e.tensor_add(
                attn_sb[:], attn_sb[:], x2_t[:]),
                waits=x2b.r() + ab.w(), inc="gpe")
            x2b.used(rv); ab.set_ready(rv)
            nj, nqs = (J, qs + 1) if qs < 3 else (J + 1, 0)
            if nj < NTB:
                rv = p.emit("gpsimd", lambda e, nj=nj, nqs=nqs: e.dma_start(
                    out=x2_t[:],
                    in_=d_x[nj * TB + nqs * 128: nj * TB + (nqs + 1) * 128, :]),
                    waits=x2b.w(), inc="x2ld", inck=16)
                x2b.set_ready(rv)
            rv = p.emit("gpsimd", lambda e, J=J, qs=qs: e.dma_start(
                out=s_y[J * TB + qs * 128: J * TB + (qs + 1) * 128, :],
                in_=attn_sb[:]),
                waits=ab.r(), inc="yst", inck=16)
            ab.used(rv)
            if "y" in d_taps:
                rv = p.emit("sync", lambda e, J=J, qs=qs: e.dma_start(
                    out=d_taps["y"][J * TB + qs * 128: J * TB + (qs + 1) * 128, :],
                    in_=attn_sb[:]), waits=ab.r(), inc="tapst", inck=16)
                ab.used(rv)

        def mk_wo_chain(J, qs, ds):
            def go():
                wob = B("wo_ps")
                rv = None
                for cc in range(CCN):
                    last = cc == CCN - 1
                    rv = p.emit("tensor", lambda e, cc=cc, qs=qs, ds=ds: e.matmul(
                        wo_ps[:],
                        ctx_sb[:, cc * TB + qs * 128: cc * TB + (qs + 1) * 128],
                        wo_sb[:, cc * C + ds * 512: cc * C + (ds + 1) * 512],
                        start=(cc == 0), stop=last),
                        waits=(wob.w() if cc == 0 else []) + B(f"ctx_h{cc}").r() +
                              (B("wo_sb").r() if cc == 0 else []),
                        inc=("pe" if last else None))
                    B(f"ctx_h{cc}").used(rv)
                wob.set_ready(rv)
                ab = B("attn")
                rv = p.emit("vector", lambda e, ds=ds: e.tensor_add(
                    attn_sb[:, ds * 512:(ds + 1) * 512], wo_ps[:],
                    bobc[:, ds * 512:(ds + 1) * 512]),
                    waits=wob.r() + B("bobc").r() + (ab.w() if ds == 0 else []),
                    inc="dve")
                wob.used(rv)
                if ds == 3:
                    ab.set_ready(rv)
                    emit_epilogue(J, qs)
            return go

        for J in range(NTB):
            nP = 2 * (J + 1)
            for hp in range(0, NH, 2):
                heads = (hp, hp + 1)

                def emit_spair(i, h, s):
                    qb = B(f"q_t{h % 2}")
                    g = h // 4
                    rv = None
                    for half in range(2):
                        kt = 2 * i + half
                        hb = B(f"sp{s}h{half}")
                        rv = p.emit("tensor", lambda e, i=i, kt=kt, g=g, h=h, half=half,
                                    s=s: e.matmul(
                            sp2[s][:, half * TB:(half + 1) * TB],
                            k_sb[:, g * T + kt * 128: g * T + (kt + 1) * 128],
                            q_t[h % 2][:], start=True, stop=True),
                            waits=hb.w() + qb.r() + B("k_sb").r(), inc="pe")
                        qb.used(rv); hb.set_ready(rv)
                    return rv

                def emit_exp(i, h, s):
                    ptb = B(f"pt2_{s}_{i % 2}")
                    hb0, hb1 = B(f"sp{s}h0"), B(f"sp{s}h1")
                    rv = p.emit("scalar", lambda e, i=i, s=s: e.activation(
                        pt2[s][i % 2][:], sp2[s][:], AF.Exp, bias=c_es[:]),
                        waits=hb0.r() + hb1.r() + ptb.w() + [cst_rv], inc="act")
                    hb0.used(rv); hb1.used(rv); ptb.set_ready(rv)
                    if i >= nP - 2:
                        mi = i - (nP - 2)
                        rv = p.emit("vector", lambda e, i=i, mi=mi, s=s: e.tensor_mul(
                            pt2[s][i % 2][:], pt2[s][i % 2][:],
                            masks_sb[:, mi * 2 * TB:(mi + 1) * 2 * TB]),
                            waits=ptb.w() + [cld_rv], inc="dve")
                        ptb.set_ready(rv)
                    return pt2[s][i % 2], ptb

                def emit_acc(i, s, peff, peffb):
                    pa = B(f"pacc2_{s}")
                    if i == 0:
                        rv = p.emit("vector", lambda e, s=s, peff=peff: e.tensor_copy(
                            pacc2[s][:], peff[:]),
                            waits=peffb.r() + pa.w(), inc="dve")
                    else:
                        rv = p.emit("vector", lambda e, s=s, peff=peff: e.tensor_tensor(
                            pacc2[s][:], pacc2[s][:], peff[:], ALU.add),
                            waits=peffb.r() + pa.w(), inc="dve")
                    peffb.used(rv); pa.set_ready(rv)

                def emit_pv(i, h, s, peff, peffb):
                    cxb = B(f"ctx_ps{h % 2}")
                    g = h // 4
                    rv = None
                    for half in range(2):
                        kt = 2 * i + half
                        first = i == 0 and half == 0
                        last = i == nP - 1 and half == 1
                        rv = p.emit("tensor", lambda e, kt=kt, g=g, h=h, half=half,
                                    peff=peff, first=first, last=last: e.matmul(
                            ctx_ps[h % 2][:],
                            v_sb[:, kt * 512 + g * 128: kt * 512 + (g + 1) * 128],
                            peff[:, half * TB:(half + 1) * TB],
                            start=first, stop=last),
                            waits=(cxb.w() if first else []) + peffb.r() + B("v_sb").r(),
                            inc=("pe" if last else None))
                        peffb.used(rv)
                    if rv is not None:
                        cxb.set_ready(rv)

                prev = {0: None, 1: None}
                for i in range(nP):
                    for s, h in enumerate(heads):
                        emit_spair(i, h, s)
                    if i == 0 and hp > 0:
                        emit_tail(J, hp - 2, 0)
                        emit_tail(J, hp - 1, 1)
                    for s, h in enumerate(heads):
                        pe_pb = emit_exp(i, h, s)
                        emit_acc(i, s, *pe_pb)
                        if prev[s] is not None:
                            emit_pv(prev[s][0], h, s, prev[s][1], prev[s][2])
                        prev[s] = (i,) + pe_pb
                    for _ in range(2 if J <= 1 else 1):
                        drain_one()
                for s, h in enumerate(heads):
                    emit_pv(prev[s][0], h, s, prev[s][1], prev[s][2])
                for s, h in enumerate(heads):
                    if h + 2 < NH:
                        emit_qload(J, h + 2)
                    elif J + 1 < NTB:
                        emit_qload(J + 1, h + 2 - NH)
                drain_one()
            emit_tail(J, NH - 2, 0)
            emit_tail(J, NH - 1, 1)
            for pr in range(2):
                stgs = [B(f"ctxstg_h{2 * pr}"), B(f"ctxstg_h{2 * pr + 1}")]
                dsts = [B(f"ctx_h{2 * pr}"), B(f"ctx_h{2 * pr + 1}")]
                rv = p.emit("gpsimd", lambda e, pr=pr: e.tensor_copy(
                    ctx_sb[:, 2 * pr * TB:(2 * pr + 2) * TB],
                    ctx_stg[:, 2 * pr * TB:(2 * pr + 2) * TB]),
                    waits=stgs[0].r() + stgs[1].r() + dsts[0].w() + dsts[1].w(),
                    inc="gpe")
                for b_ in stgs:
                    b_.used(rv)
                for b_ in dsts:
                    b_.set_ready(rv)
            for qs in range(4):
                for ds in range(4):
                    wo_pending.append(mk_wo_chain(J, qs, ds))
        while wo_pending:
            drain_one()

        yst_rv = p.val("yst")
        p.block("phase2")

    with contextlib.ExitStack() as stk3o:
        rmb_sb = stk3o.enter_context(nc.sbuf_tensor("rmb_sb", [EH, E], F32))
        rtb_sb = stk3o.enter_context(nc.sbuf_tensor("rtb_sb", [EH, E], F32))
        with contextlib.ExitStack() as stk3a:
            y3_t = [stk3a.enter_context(nc.sbuf_tensor(f"y3_t{i}", [128, C], F32)) for i in range(2)]
            ln3_sb = [stk3a.enter_context(nc.sbuf_tensor(f"ln3_sb{i}", [128, C], F32)) for i in range(2)]
            yn_bf = [stk3a.enter_context(nc.sbuf_tensor(f"yn_bf{i}", [128, C], BF16)) for i in range(2)]
            sq3 = stk3a.enter_context(nc.sbuf_tensor("sq3", [128, C], F32))
            seq_ps = [stk3a.enter_context(nc.psum_tensor(f"seq_ps{i}", [1, 512], F32)) for i in range(4)]

            for ti in range(NT128):
                par = ti % 2
                yb = B(f"y3_{par}")
                rv = p.emit("sync", lambda e, ti=ti, par=par: e.dma_start(
                    out=y3_t[par][:], in_=s_y[ti * 128:(ti + 1) * 128, :]),
                    waits=[yst_rv] + yb.w(), inc=f"yld{par}", inck=16)
                yb.set_ready(rv)
                t_rstd, b_rstd, t_nb, b_nb = ln_stats(
                    "l3", par, y3_t[par][:], yb, sq3[:], B("sq3"))
                l3b = B(f"ln3_{par}")
                rv = p.emit("scalar", lambda e, par=par, r=t_rstd, nb=t_nb: e.activation(
                    ln3_sb[par][:], y3_t[par][:], AF.Identity, bias=nb[:], scale=r[:]),
                    waits=yb.r() + b_rstd.r() + b_nb.r() + l3b.w(), inc="act")
                yb.used(rv); b_rstd.used(rv); b_nb.used(rv); l3b.set_ready(rv)
                ynb = B(f"ynbf{par}")
                rv = p.emit("vector", lambda e, par=par: e.tensor_copy(
                    yn_bf[par][:], ln3_sb[par][:]),
                    waits=l3b.r() + ynb.w(), inc="dve")
                l3b.used(rv); ynb.set_ready(rv)
                rv = p.emit("gpsimd", lambda e, ti=ti, par=par: e.dma_start(
                    out=s_yn[ti * 128:(ti + 1) * 128, :], in_=yn_bf[par][:]),
                    waits=ynb.r(), inc=f"ynst{par}", inck=16)
                ynb.used(rv)
                if "yn" in d_taps:
                    rv = p.emit("sync", lambda e, ti=ti, par=par: e.dma_start(
                        out=d_taps["yn"][ti * 128:(ti + 1) * 128, :], in_=yn_bf[par][:]),
                        waits=ynb.r(), inc="tapst", inck=16)
                    ynb.used(rv)
                for s4 in range(4):
                    rv = p.emit("tensor", lambda e, ti=ti, par=par, s4=s4: e.matmul(
                        seq_ps[s4][:], ones128[:],
                        ln3_sb[par][:, s4 * 512:(s4 + 1) * 512],
                        start=(ti == 0), stop=(ti == NT128 - 1)),
                        waits=l3b.r() + [cst_rv], inc=("pe" if s4 == 3 else None))
                    l3b.used(rv)
            ynst_rv = [p.val("ynst0"), p.val("ynst1")]
            seq_done = p.val("pe")

            seq_row = stk3a.enter_context(nc.sbuf_tensor("seq_row", [1, C], F32))
            sqg = stk3a.enter_context(nc.sbuf_tensor("sqg", [1, C], F32))
            sim_sb = stk3a.enter_context(nc.sbuf_tensor("sim_sb", [128, CCN * E], F32))
            sim_sq = stk3a.enter_context(nc.sbuf_tensor("sim_sq", [128, CCN * E], F32))
            seq_fm = stk3a.enter_context(nc.sbuf_tensor("seq_fm", [128, CCN], F32))
            g_pool = stk3a.enter_context(nc.sbuf_tensor("g_pool", [1, 32 * E], F32))
            g32 = stk3a.enter_context(nc.sbuf_tensor("g32", [E, E], F32))
            e32 = stk3a.enter_context(nc.sbuf_tensor("e32", [E, E], F32))
            e32b = stk3a.enter_context(nc.sbuf_tensor("e32b", [E, E], F32))
            r32 = stk3a.enter_context(nc.sbuf_tensor("r32", [E, E], F32))
            lb_sb = stk3a.enter_context(nc.sbuf_tensor("lb_sb", [E, E], F32))
            lg_col = stk3a.enter_context(nc.sbuf_tensor("lg_col", [E, 1], F32))
            rk_col = stk3a.enter_context(nc.sbuf_tensor("rk_col", [E, 1], F32))
            fb_col = stk3a.enter_context(nc.sbuf_tensor("fb_col", [E, 1], F32))
            t11 = stk3a.enter_context(nc.sbuf_tensor("t11", [1, 4], F32))
            g_ps = stk3a.enter_context(nc.psum_tensor("g_ps", [128, 512], F32))
            gp32 = g_ps[0:E, 0:E]
            gp1e = g_ps[0:1, 0:E]
            gp64 = g_ps[0:EH, 0:E]

            def grow(i):
                return g_pool[:, i * E:(i + 1) * E]

            sqb = B("seq_row")
            for s4 in range(4):
                p.emit("scalar", lambda e, s4=s4: e.activation(
                    seq_row[:, s4 * 512:(s4 + 1) * 512], seq_ps[s4][:], AF.Copy),
                    waits=[seq_done], inc="act")
            sqb.set_ready(p.val("act"))
            ssb = B("ss")
            rv = p.emit("scalar", lambda e: e.activation(
                sqg[:], seq_row[:], AF.Square, accum_out=t11[:, 0:1]),
                waits=sqb.r() + ssb.w(), inc="act")
            sqb.used(rv); ssb.set_ready(rv)
            rv = p.emit("sync", lambda e: e.dma_start(out=s_seq[:], in_=seq_row[:]),
                        waits=sqb.r(), inc="gsq", inck=16)
            sqb.used(rv)
            seq_st = p.val("gsq")
            sfmb = B("seq_fm")
            for cc in range(CCN):
                p.emit("sync", lambda e, cc=cc: e.dma_start(
                    out=seq_fm[:, cc:cc + 1],
                    in_=s_seq[:, cc * 128:(cc + 1) * 128].rearrange("1 p -> p 1")),
                    waits=[seq_st] + (sfmb.w() if cc == 0 else []), inc="gfm", inck=16)
            sfmb.set_ready(p.val("gfm"))
            simb = B("sim_sb")
            for cc in range(CCN):
                p.emit("sync", lambda e, cc=cc: e.dma_start(
                    out=sim_sb[:, cc * E:(cc + 1) * E],
                    in_=d_sim[cc * 128:(cc + 1) * 128, :]),
                    inc="gsim", inck=16)
            simb.set_ready(p.val("gsim"))
            simq = B("sim_sq")
            rv = p.emit("scalar", lambda e: e.activation(sim_sq[:], sim_sb[:], AF.Square),
                        waits=simb.r() + simq.w(), inc="act")
            simb.used(rv); simq.set_ready(rv)
            csb = B("cs_ps")
            rv = None
            for cc in range(CCN):
                last = cc == CCN - 1
                rv = p.emit("tensor", lambda e, cc=cc: e.matmul(
                    gp1e, ones128[:], sim_sq[:, cc * E:(cc + 1) * E],
                    start=(cc == 0), stop=(cc == CCN - 1)),
                    waits=simq.r() + [cst_rv] + (csb.w() if cc == 0 else []),
                    inc=("pe" if last else None))
                simq.used(rv)
            csb.set_ready(rv)
            cs_sb = B("cs_sb")
            rv = p.emit("scalar", lambda e: e.activation(grow(0), gp1e, AF.Copy),
                        waits=csb.r() + cs_sb.w(), inc="act")
            csb.used(rv); cs_sb.set_ready(rv)
            rawb = B("raw_ps")
            rv = None
            for cc in range(CCN):
                last = cc == CCN - 1
                rv = p.emit("tensor", lambda e, cc=cc: e.matmul(
                    gp1e, seq_fm[:, cc:cc + 1], sim_sb[:, cc * E:(cc + 1) * E],
                    start=(cc == 0), stop=(cc == CCN - 1)),
                    waits=sfmb.r() + simb.r() + (rawb.w() + csb.w() if cc == 0 else []),
                    inc=("pe" if last else None))
                sfmb.used(rv); simb.used(rv)
            rawb.set_ready(rv)
            raw_sb = B("raw_sb")
            rv = p.emit("scalar", lambda e: e.activation(grow(1), gp1e, AF.Copy),
                        waits=rawb.r() + raw_sb.w(), inc="act")
            rawb.used(rv); raw_sb.set_ready(rv)
            lssb = B("lss")
            rv = p.emit("scalar", lambda e: e.activation(t11[:, 1:2], t11[:, 0:1], AF.Ln),
                        waits=ssb.r() + lssb.w(), inc="act")
            ssb.used(rv); lssb.set_ready(rv)
            lcsb = B("lcs")
            rv = p.emit("scalar", lambda e: e.activation(grow(2), grow(0), AF.Ln),
                        waits=cs_sb.r() + lcsb.w(), inc="act")
            cs_sb.used(rv); lcsb.set_ready(rv)
            ldenb = B("lden")
            rv = p.emit("vector", lambda e: e.tensor_scalar_add(grow(3), grow(2), t11[:, 1:2]),
                        waits=lssb.r() + lcsb.r() + ldenb.w(), inc="dve")
            lssb.used(rv); lcsb.used(rv); ldenb.set_ready(rv)
            rdenb = B("rden")
            rv = p.emit("scalar", lambda e: e.activation(
                grow(4), grow(3), AF.Exp, scale=c_mhalf[0:1, :]),
                waits=ldenb.r() + rdenb.w() + [cst_rv], inc="act")
            ldenb.used(rv); rdenb.set_ready(rv)
            affb = B("aff")
            rv = p.emit("vector", lambda e: e.tensor_mul(grow(5), grow(1), grow(4)),
                        waits=raw_sb.r() + rdenb.r() + affb.w(), inc="dve")
            raw_sb.used(rv); rdenb.used(rv); affb.set_ready(rv)
            sgb = B("sg")
            rv = p.emit("scalar", lambda e: e.activation(grow(6), gates_sb[:], AF.Sigmoid),
                        waits=[cld_rv] + sgb.w(), inc="act")
            sgb.set_ready(rv)
            lgb = B("lg")
            rv = p.emit("vector", lambda e: e.tensor_sub(grow(7), grow(5), grow(6)),
                        waits=affb.r() + sgb.r() + lgb.w(), inc="dve")
            affb.used(rv); sgb.used(rv); lgb.set_ready(rv)
            gdb = B("gd")
            rv = p.emit("vector", lambda e: e.tensor_scalar_max(grow(8), grow(7), 0.0),
                        waits=lgb.r() + gdb.w(), inc="dve")
            lgb.used(rv); gdb.set_ready(rv)
            mkb = B("mk")
            rv = p.emit("vector", lambda e: e.tensor_single_scalar(
                grow(9), grow(7), 0.0, ALU.is_gt),
                waits=lgb.r() + mkb.w(), inc="dve")
            lgb.used(rv); mkb.set_ready(rv)
            nab = B("na")
            rv = p.emit("vector", lambda e: e.reduce_sum(t11[:, 2:3], grow(9), axis=AX),
                        waits=mkb.r() + nab.w(), inc="dve")
            mkb.used(rv); nab.set_ready(rv)
            iab = B("ia")
            rv = p.emit("vector", lambda e: e.tensor_single_scalar(
                t11[:, 3:4], t11[:, 2:3], 0.5, ALU.is_lt),
                waits=nab.r() + iab.w(), inc="dve")
            nab.used(rv); iab.set_ready(rv)
            rv = p.emit("sync", lambda e: e.dma_start(out=s_lg[:], in_=grow(7)),
                        waits=lgb.r(), inc="glg", inck=16)
            lgb.used(rv)
            lg_st = p.val("glg")
            lcolb = B("lg_col")
            rv = p.emit("sync", lambda e: e.dma_start(
                out=lg_col[:], in_=s_lg.rearrange("1 e -> e 1")),
                waits=[lg_st] + lcolb.w(), inc="glgc", inck=16)
            lcolb.set_ready(rv)
            lbb = B("lb_ps")
            rv = p.emit("tensor", lambda e: e.matmul(
                gp32, ones1[:, 0:E], grow(7), start=True, stop=True),
                waits=lgb.r() + [cst_rv] + rawb.w() + csb.w(), inc="pe")
            lgb.used(rv); lbb.set_ready(rv)
            lbsb = B("lb_sb")
            rv = p.emit("scalar", lambda e: e.activation(lb_sb[:], gp32, AF.Copy),
                        waits=lbb.r() + lbsb.w(), inc="act")
            lbb.used(rv); lbsb.set_ready(rv)
            g32b = B("g32")
            rv = p.emit("vector", lambda e: e.tensor_single_scalar(
                g32[:], lb_sb[:], lg_col[:], ALU.is_gt),
                waits=lbsb.r() + lcolb.r() + g32b.w(), inc="dve")
            lbsb.used(rv); lcolb.used(rv); g32b.set_ready(rv)
            e32bf = B("e32")
            rv = p.emit("vector", lambda e: e.tensor_single_scalar(
                e32[:], lb_sb[:], lg_col[:], ALU.is_equal),
                waits=lbsb.r() + lcolb.r() + e32bf.w(), inc="dve")
            lbsb.used(rv); lcolb.used(rv); e32bf.set_ready(rv)
            e32bb = B("e32b")
            rv = p.emit("vector", lambda e: e.tensor_mul(e32b[:], e32[:], lt32_sb[:]),
                        waits=e32bf.r() + [cld_rv] + e32bb.w(), inc="dve")
            e32bf.used(rv); e32bb.set_ready(rv)
            r32b = B("r32")
            rv = p.emit("vector", lambda e: e.tensor_add(r32[:], g32[:], e32b[:]),
                        waits=g32b.r() + e32bb.r() + r32b.w(), inc="dve")
            g32b.used(rv); e32bb.used(rv); r32b.set_ready(rv)
            rkb = B("rk")
            rv = p.emit("vector", lambda e: e.reduce_sum(rk_col[:], r32[:], axis=AX),
                        waits=r32b.r() + rkb.w(), inc="dve")
            r32b.used(rv); rkb.set_ready(rv)
            fcb = B("fb_col")
            rv = p.emit("vector", lambda e: e.tensor_single_scalar(
                fb_col[:], rk_col[:], float(E // 2) - 0.5, ALU.is_lt),
                waits=rkb.r() + fcb.w(), inc="dve")
            rkb.used(rv); fcb.set_ready(rv)
            rv = p.emit("sync", lambda e: e.dma_start(
                out=s_fb.rearrange("1 e -> e 1"), in_=fb_col[:]),
                waits=fcb.r(), inc="gfb", inck=16)
            fcb.used(rv)
            fb_st = p.val("gfb")
            frb = B("fb_row")
            rv = p.emit("sync", lambda e: e.dma_start(out=grow(10), in_=s_fb[:]),
                        waits=[fb_st] + frb.w(), inc="gfbr", inck=16)
            frb.set_ready(rv)
            d1b = B("d1")
            rv = p.emit("vector", lambda e: e.tensor_sub(grow(11), grow(10), grow(9)),
                        waits=frb.r() + mkb.r() + d1b.w(), inc="dve")
            frb.used(rv); mkb.used(rv); d1b.set_ready(rv)
            d2b = B("d2")
            rv = p.emit("vector", lambda e: e.tensor_single_scalar(
                grow(12), grow(11), t11[:, 3:4], ALU.mult),
                waits=d1b.r() + iab.r() + d2b.w(), inc="dve")
            d1b.used(rv); iab.used(rv); d2b.set_ready(rv)
            mkfb = B("mkf")
            rv = p.emit("vector", lambda e: e.tensor_add(grow(13), grow(9), grow(12)),
                        waits=mkb.r() + d2b.r() + mkfb.w(), inc="dve")
            mkb.used(rv); d2b.used(rv); mkfb.set_ready(rv)
            z1b = B("z1")
            rv = p.emit("vector", lambda e: e.tensor_mul(grow(14), grow(8), grow(13)),
                        waits=gdb.r() + mkfb.r() + z1b.w(), inc="dve")
            gdb.used(rv); mkfb.used(rv); z1b.set_ready(rv)
            z2b = B("z2")
            rv = p.emit("vector", lambda e: e.tensor_scalar(
                grow(15), grow(13), -1.0, NEGMAX, ALU.add, ALU.mult),
                waits=mkfb.r() + z2b.w(), inc="dve")
            mkfb.used(rv); z2b.set_ready(rv)
            z3b = B("z3")
            rv = p.emit("vector", lambda e: e.tensor_add(grow(16), grow(14), grow(15)),
                        waits=z1b.r() + z2b.r() + z3b.w(), inc="dve")
            z1b.used(rv); z2b.used(rv); z3b.set_ready(rv)
            zmb = B("zm")
            rv = p.emit("vector", lambda e: e.reduce_max(t11[:, 0:1], grow(16), axis=AX),
                        waits=z3b.r() + ssb.w() + zmb.w(), inc="dve")
            z3b.used(rv); zmb.set_ready(rv)
            zmnb = B("zmn")
            rv = p.emit("vector", lambda e: e.tensor_scalar_mul(t11[:, 1:2], t11[:, 0:1], -1.0),
                        waits=zmb.r() + lssb.w() + zmnb.w(), inc="dve")
            zmb.used(rv); zmnb.set_ready(rv)
            ezb = B("ez")
            rv = p.emit("scalar", lambda e: e.activation(
                grow(17), grow(16), AF.Exp, bias=t11[:, 1:2], accum_out=t11[:, 2:3]),
                waits=z3b.r() + zmnb.r() + nab.w() + ezb.w(), inc="act")
            z3b.used(rv); zmnb.used(rv); ezb.set_ready(rv)
            zrb = B("zr")
            rv = p.emit("vector", lambda e: e.reciprocal(t11[:, 3:4], t11[:, 2:3]),
                        waits=ezb.r() + iab.w() + zrb.w(), inc="dve")
            ezb.used(rv); zrb.set_ready(rv)
            rtb = B("rt")
            rv = p.emit("vector", lambda e: e.tensor_single_scalar(
                grow(18), grow(17), t11[:, 3:4], ALU.mult),
                waits=ezb.r() + zrb.r() + rtb.w(), inc="dve")
            ezb.used(rv); zrb.used(rv); rtb.set_ready(rv)
            rmb = B("rm")
            rv = p.emit("vector", lambda e: e.tensor_mul(grow(19), grow(18), grow(13)),
                        waits=rtb.r() + mkfb.r() + rmb.w(), inc="dve")
            rtb.used(rv); mkfb.used(rv); rmb.set_ready(rv)
            rmbp = B("rmb_ps")
            rv = p.emit("tensor", lambda e: e.matmul(
                gp64, ones1[:, 0:EH], grow(19), start=True, stop=True),
                waits=rmb.r() + [cst_rv] + lbb.w(), inc="pe")
            rmb.used(rv); rmbp.set_ready(rv)
            rmsb = B("rmb_sb")
            rv = p.emit("scalar", lambda e: e.activation(rmb_sb[:], gp64, AF.Copy),
                        waits=rmbp.r() + rmsb.w(), inc="act")
            rmbp.used(rv); rmsb.set_ready(rv)
            rtbp = B("rtb_ps")
            rv = p.emit("tensor", lambda e: e.matmul(
                gp64, ones1[:, 0:EH], grow(18), start=True, stop=True),
                waits=rtb.r() + rmbp.w() + [cst_rv], inc="pe")
            rtb.used(rv); rtbp.set_ready(rv)
            rtsb = B("rtb_sb")
            rv = p.emit("scalar", lambda e: e.activation(rtb_sb[:], gp64, AF.Copy),
                        waits=rtbp.r() + rtsb.w(), inc="act")
            rtbp.used(rv); rtsb.set_ready(rv)
            if "gat" in d_taps:
                for gi, gv in enumerate([0, 5, 7, 9, 13, 10, 18, 19]):
                    p.emit("sync", lambda e, gi=gi, gv=gv: e.dma_start(
                        out=d_taps["gat"][:, gi * E:(gi + 1) * E], in_=grow(gv)),
                        waits=rmb.r() + rtb.r(), inc="tapst", inck=16)
                p.emit("sync", None, waits=[p.val("tapst")])
            p.block("phase3ab")

        weffcm = stk3o.enter_context(nc.sbuf_tensor("weffcm", [128, CCN * EH], BF16))
        wod_bf = stk3o.enter_context(nc.sbuf_tensor("wod_bf", [EH, C], BF16))
        with contextlib.ExitStack() as stk3c:
            wv_t = [stk3c.enter_context(nc.sbuf_tensor(f"wv_t{i}", [EH, C], F32)) for i in range(2)]
            ow_t = [stk3c.enter_context(nc.sbuf_tensor(f"ow_t{i}", [EH, C], F32)) for i in range(2)]
            wacc = [stk3c.enter_context(nc.sbuf_tensor(f"wacc{i}", [EH, C], F32)) for i in range(2)]
            oacc = [stk3c.enter_context(nc.sbuf_tensor(f"oacc{i}", [EH, C], F32)) for i in range(2)]
            weff_bf = stk3c.enter_context(nc.sbuf_tensor("weff_bf", [EH, C], BF16))
            wt_ps = stk3c.enter_context(nc.psum_tensor("wt_ps", [128, EH], BF16))

            for ei in range(E):
                par = ei % 2
                wvb = B(f"wv_{par}")
                rv = p.emit("sync", lambda e, ei=ei, par=par: e.dma_start(
                    out=wv_t[par][:], in_=d_wv2[ei * EH:(ei + 1) * EH, :]),
                    waits=wvb.w(), inc=f"wvld{par}", inck=16)
                wvb.set_ready(rv)
                owb = B(f"ow_{par}")
                rv = p.emit("sync", lambda e, ei=ei, par=par: e.dma_start(
                    out=ow_t[par][:], in_=d_ow2[ei * EH:(ei + 1) * EH, :]),
                    waits=owb.w(), inc=f"owld{par}", inck=16)
                owb.set_ready(rv)
                wa_in, wa_out = B(f"wacc{ei % 2}"), B(f"wacc{(ei + 1) % 2}")
                oa_in, oa_out = B(f"oacc{ei % 2}"), B(f"oacc{(ei + 1) % 2}")
                if ei == 0:
                    rv = p.emit("vector", lambda e, par=par: e.tensor_single_scalar(
                        wacc[1][:], wv_t[par][:], rmb_sb[:, 0:1], ALU.mult),
                        waits=wvb.r() + B("rmb_sb").r() + wa_out.w(), inc="dve")
                    wvb.used(rv); B("rmb_sb").used(rv); wa_out.set_ready(rv)
                    rv = p.emit("vector", lambda e, par=par: e.tensor_single_scalar(
                        oacc[1][:], ow_t[par][:], rtb_sb[:, 0:1], ALU.mult),
                        waits=owb.r() + B("rtb_sb").r() + oa_out.w(), inc="dve")
                    owb.used(rv); B("rtb_sb").used(rv); oa_out.set_ready(rv)
                else:
                    rv = p.emit("vector", lambda e, ei=ei, par=par: e.scalar_tensor_tensor(
                        wacc[(ei + 1) % 2][:], wv_t[par][:], rmb_sb[:, ei:ei + 1],
                        wacc[ei % 2][:], ALU.mult, ALU.add),
                        waits=wvb.r() + wa_in.r() + B("rmb_sb").r() + wa_out.w(), inc="dve")
                    wvb.used(rv); wa_in.used(rv); B("rmb_sb").used(rv); wa_out.set_ready(rv)
                    rv = p.emit("vector", lambda e, ei=ei, par=par: e.scalar_tensor_tensor(
                        oacc[(ei + 1) % 2][:], ow_t[par][:], rtb_sb[:, ei:ei + 1],
                        oacc[ei % 2][:], ALU.mult, ALU.add),
                        waits=owb.r() + oa_in.r() + B("rtb_sb").r() + oa_out.w(), inc="dve")
                    owb.used(rv); oa_in.used(rv); B("rtb_sb").used(rv); oa_out.set_ready(rv)
            wfin = B(f"wacc{E % 2}")
            ofin = B(f"oacc{E % 2}")
            wbfb = B("weff_bf")
            rv = p.emit("vector", lambda e: e.tensor_copy(weff_bf[:], wacc[E % 2][:]),
                        waits=wfin.r() + wbfb.w(), inc="dve")
            wfin.used(rv); wbfb.set_ready(rv)
            obfb = B("wod_bf")
            rv = p.emit("vector", lambda e: e.tensor_copy(wod_bf[:], oacc[E % 2][:]),
                        waits=ofin.r() + obfb.w(), inc="dve")
            ofin.used(rv); obfb.set_ready(rv)
            for cc in range(CCN):
                wtb = B("wt_ps")
                rv = p.emit("tensor", lambda e, cc=cc: e.transpose(
                    wt_ps[:], weff_bf[:, cc * 128:(cc + 1) * 128], ident[0:EH, 0:EH]),
                    waits=wbfb.r() + [cld_rv] + wtb.w(), inc="pe")
                wbfb.used(rv); wtb.set_ready(rv)
                rv = p.emit("scalar", lambda e, cc=cc: e.activation(
                    weffcm[:, cc * EH:(cc + 1) * EH], wt_ps[:], AF.Copy),
                    waits=wtb.r(), inc="act")
                wtb.used(rv)
            wcmb = B("weffcm")
            wcmb.set_ready(p.val("act"))
            p.block("phase3c")

        with contextlib.ExitStack() as stk3d:
            ynfm = [stk3d.enter_context(nc.sbuf_tensor(f"ynfm{i}", [128, CCN * TB], BF16)) for i in range(2)]
            comb_bf = [stk3d.enter_context(nc.sbuf_tensor(f"comb_bf{i}", [EH, TB], BF16)) for i in range(2)]
            ffn_sb = [stk3d.enter_context(nc.sbuf_tensor(f"ffn_sb{i}", [128, C], F32)) for i in range(2)]
            lnf_sb = stk3d.enter_context(nc.sbuf_tensor("lnf_sb", [128, C], F32))
            out_sb = [stk3d.enter_context(nc.sbuf_tensor(f"out_sb{i}", [128, C], F32)) for i in range(2)]
            y4_t = [stk3d.enter_context(nc.sbuf_tensor(f"y4_t{i}", [128, C], F32)) for i in range(2)]
            sq4 = stk3d.enter_context(nc.sbuf_tensor("sq4", [128, C], F32))
            comb_ps = [stk3d.enter_context(nc.psum_tensor(f"comb_ps{i}", [EH, TB], F32)) for i in range(2)]
            f_ps = [stk3d.enter_context(nc.psum_tensor(f"f_ps{i}", [128, 512], F32)) for i in range(2)]

            for b in range(NTB):
                pb = b % 2
                fmb = B(f"ynfm{pb}")
                for cc in range(CCN):
                    p.emit("sync", lambda e, b=b, cc=cc, pb=pb: e.dma_start(
                        out=ynfm[pb][:, cc * TB:(cc + 1) * TB],
                        in_=s_yn[b * TB:(b + 1) * TB, cc * 128:(cc + 1) * 128],
                        transpose=True),
                        waits=ynst_rv + (fmb.w() if cc == 0 else []),
                        inc=f"ynfl{pb}", inck=16)
                fmb.set_ready(p.val(f"ynfl{pb}"))
                cpb = B(f"comb_ps{pb}")
                rv = None
                for cc in range(CCN):
                    last = cc == CCN - 1
                    rv = p.emit("tensor", lambda e, cc=cc, pb=pb: e.matmul(
                        comb_ps[pb][:], weffcm[:, cc * EH:(cc + 1) * EH],
                        ynfm[pb][:, cc * TB:(cc + 1) * TB],
                        start=(cc == 0), stop=(cc == CCN - 1)),
                        waits=fmb.r() + wcmb.r() + (cpb.w() if cc == 0 else []),
                        inc=("pe" if last else None))
                fmb.used(rv); wcmb.used(rv)
                cpb.set_ready(rv)
                cbb = B(f"comb_bf{pb}")
                rv = p.emit("scalar", lambda e, pb=pb: e.activation(
                    comb_bf[pb][:], comb_ps[pb][:], AF.Copy),
                    waits=cpb.r() + cbb.w(), inc="act")
                cpb.used(rv); cbb.set_ready(rv)
                for ts in range(4):
                    par = ts % 2
                    ti = b * 4 + ts
                    y4b = B(f"y4_{par}")
                    rv = p.emit("gpsimd", lambda e, ti=ti, par=par: e.dma_start(
                        out=y4_t[par][:], in_=s_y[ti * 128:(ti + 1) * 128, :]),
                        waits=[yst_rv] + y4b.w(), inc=f"y4ld{par}", inck=16)
                    y4b.set_ready(rv)
                    fb_ = B(f"ffn{par}")
                    for ds in range(4):
                        fp = f_ps[(ts * 4 + ds) % 2]
                        fpb = B(f"f_ps{(ts * 4 + ds) % 2}")
                        rv = p.emit("tensor", lambda e, ts=ts, ds=ds, pb=pb, fp=fp: e.matmul(
                            fp[:], comb_bf[pb][:, ts * 128:(ts + 1) * 128],
                            wod_bf[:, ds * 512:(ds + 1) * 512], start=True, stop=True),
                            waits=cbb.r() + obfb.r() + fpb.w(), inc="pe")
                        cbb.used(rv); obfb.used(rv); fpb.set_ready(rv)
                        rv = p.emit("vector", lambda e, par=par, ds=ds, fp=fp: e.tensor_copy(
                            ffn_sb[par][:, ds * 512:(ds + 1) * 512], fp[:]),
                            waits=fpb.r() + (fb_.w() if ds == 0 else []), inc="dve")
                        fpb.used(rv)
                    fb_.set_ready(p.val("dve"))
                    t_rstd, b_rstd, t_nb, b_nb = ln_stats(
                        "l4", par, ffn_sb[par][:], fb_, sq4[:], B("sq4"))
                    lfb = B("lnf")
                    rv = p.emit("scalar", lambda e, par=par, r=t_rstd, nb=t_nb: e.activation(
                        lnf_sb[:], ffn_sb[par][:], AF.Identity, bias=nb[:], scale=r[:]),
                        waits=fb_.r() + b_rstd.r() + b_nb.r() + lfb.w(), inc="act")
                    fb_.used(rv); b_rstd.used(rv); b_nb.used(rv); lfb.set_ready(rv)
                    ob = B(f"out{par}")
                    rv = p.emit("vector", lambda e, par=par: e.tensor_add(
                        out_sb[par][:], lnf_sb[:], y4_t[par][:]),
                        waits=lfb.r() + y4b.r() + ob.w(), inc="dve")
                    lfb.used(rv); y4b.used(rv); ob.set_ready(rv)
                    rv = p.emit("gpsimd", lambda e, ti=ti, par=par: e.dma_start(
                        out=d_out[ti * 128:(ti + 1) * 128, :], in_=out_sb[par][:]),
                        waits=ob.r(), inc="outst", inck=16)
                    ob.used(rv)

            p.emit("sync", None, waits=[p.val("outst")])
            p.block("phase3d")
    return nc


def make_masks():
    m = np.zeros((128, 4 * TB), dtype=np.float32)
    for o in range(4):
        off = o * 128
        for pp in range(128):
            m[pp, o * TB:(o + 1) * TB] = (np.arange(TB) >= (pp + off)).astype(np.float32)
    return m.astype(BFNP)


def make_inputs_per_core(xb, Wq, Wk, Wv, Wo, bo, sim, gates, wv2, ow2):
    lt = np.tril(np.ones((E, E), np.float32), -1)
    def chunkw(W):
        W = np.asarray(W, dtype=np.float32)
        cc = W.shape[0] // 128
        return np.ascontiguousarray(
            np.transpose(W.reshape(cc, 128, W.shape[1]), (1, 0, 2)).reshape(128, -1))
    return {
        "x": np.ascontiguousarray(xb, dtype=np.float32),
        "x_fm": np.ascontiguousarray(np.asarray(xb, dtype=np.float32).T),
        "Wq2": chunkw(Wq),
        "Wk2": chunkw(Wk),
        "Wv2": chunkw(Wv),
        "Wo2": chunkw(Wo),
        "bo": np.ascontiguousarray(np.reshape(bo, (1, C)), dtype=np.float32),
        "sim": np.ascontiguousarray(sim, dtype=np.float32),
        "gates": np.ascontiguousarray(np.reshape(gates, (1, E)), dtype=np.float32),
        "wv2": np.ascontiguousarray(np.reshape(wv2, (E * EH, C)), dtype=np.float32),
        "ow2": np.ascontiguousarray(np.reshape(ow2, (E * EH, C)), dtype=np.float32),
        "identw": np.eye(128, dtype=BFNP),
        "masks": make_masks(),
        "lt32": lt,
    }


def kernel(L0_attn_x, L0_attn_Wq, L0_attn_Wk, L0_attn_Wv, L0_attn_Wo, L0_attn_bo,
           L0_ffn_sim_matrix, L0_ffn_gates, L0_ffn_w_q, L0_ffn_w_k, L0_ffn_w_v,
           L0_ffn_o_weights):
    x = np.asarray(L0_attn_x)
    nb = x.shape[0]
    nc = build_nc(T=x.shape[1])
    in_maps = []
    for b in range(nb):
        in_maps.append(make_inputs_per_core(
            x[b], np.asarray(L0_attn_Wq), np.asarray(L0_attn_Wk),
            np.asarray(L0_attn_Wv), np.asarray(L0_attn_Wo), np.asarray(L0_attn_bo),
            np.asarray(L0_ffn_sim_matrix), np.asarray(L0_ffn_gates),
            np.asarray(L0_ffn_w_v), np.asarray(L0_ffn_o_weights)))
    res = run_bass_kernel_spmd(nc, in_maps, core_ids=list(range(nb)))
    out = np.stack([np.asarray(r["out"]) for r in res.results], axis=0)
    return out.astype(np.float32)
